# version 27
# baseline (speedup 1.0000x reference)
import numpy as np
from contextlib import ExitStack

import concourse.bass as bass
import concourse.mybir as mybir
from concourse.bass_utils import run_bass_kernel_spmd

F32 = mybir.dt.float32
BF16 = mybir.dt.bfloat16
AF = mybir.ActivationFunctionType
ALU = mybir.AluOpType
AX = mybir.AxisListType

B, S, H, NH, HD, INTER = 4, 2048, 2048, 16, 128, 5632
SCALE = float(HD) ** -0.5
EPS = 1e-5
HC = H // 128
IC = INTER // 128
NKC = (8, 16)
R_P = 4

A_XN = 0
A_Q = 16384
A_AT = 32768
A_MSK = 49152
A_HDN = 32768
A_K = 57344
A_V = 59392
A_PR = 61440
A_BC = 63488
ARENA_E = 67584

ENGS = ("sync", "gpsimd", "tensor", "scalar", "vector")



def build(debug=False, dbg_tap=None):
    nc = bass.Bass("TRN2", target_bir_lowering=False)

    d_xf_g = nc.dram_tensor("xf_g", [H, S], F32, kind="ExternalInput").ap()
    d_xtm_g = nc.dram_tensor("xtm_g", [S, H], F32, kind="ExternalInput").ap()
    d_xf_o = nc.dram_tensor("xf_o", [H, 1024], F32, kind="ExternalInput").ap()
    d_xtm_o = nc.dram_tensor("xtm_o", [1024, H], F32, kind="ExternalInput").ap()
    d_wq = nc.dram_tensor("wq_t", [18, 128, 2048], F32, kind="ExternalInput").ap()
    d_bqkv = nc.dram_tensor("bqkv_t", [128, 18], F32, kind="ExternalInput").ap()
    d_wp = nc.dram_tensor("wp_t", [16, 128, 2048], F32, kind="ExternalInput").ap()
    d_bproj = nc.dram_tensor("bproj_b", [128, 2048], F32, kind="ExternalInput").ap()
    d_gu = nc.dram_tensor("gu_t", [44, 128, 4096], F32, kind="ExternalInput").ap()
    d_gs = nc.dram_tensor("gs_t", [128, 44], F32, kind="ExternalInput").ap()
    d_us = nc.dram_tensor("us_t", [128, 44], F32, kind="ExternalInput").ap()
    d_dn = nc.dram_tensor("dn_t", [2, 44, 128, 1024], F32, kind="ExternalInput").ap()
    d_dsb = nc.dram_tensor("dsb_b", [128, 2048], F32, kind="ExternalInput").ap()
    d_msk = nc.dram_tensor("masks", [128, 8192], F32, kind="ExternalInput").ap()
    d_id = nc.dram_tensor("ident", [128, 128], F32, kind="ExternalInput").ap()
    d_out = nc.dram_tensor("out", [1024, H], F32, kind="ExternalOutput").ap()
    dbg_t = {}
    if debug:
        for nm, width in [("dbg_xn", 16384), ("dbg_q", 16384), ("dbg_k", 2048),
                          ("dbg_v", 2048), ("dbg_at", 16384), ("dbg_s", 16384),
                          ("dbg_hd", 22528)]:
            dbg_t[nm] = nc.dram_tensor(nm, [128, width], BF16,
                                       kind="ExternalOutput").ap()

    es = ExitStack()
    sb = lambda nm, shp, dt: es.enter_context(nc.sbuf_tensor(nm, shp, dt))
    pt = lambda nm, shp, dt: es.enter_context(nc.psum_tensor(nm, shp, dt))

    arena = sb("arena", [128, ARENA_E], BF16)
    wring = sb("wring", [128, 8192], BF16)
    wd = sb("wd", [128, 3072], BF16)
    xtm = sb("xtm", [128, 2 * 2048], F32)
    xf = sb("xf", [128, 4 * 512], F32)
    fscr = sb("fscr", [128, 8 * 512], F32)
    outt = sb("outt", [128, 2 * 512], F32)
    st24 = outt[:, 0:96]
    stat = outt[:, 96:224]
    rrow = sb("rrow", [1, 2048], F32)
    bqkv = sb("bqkv", [128, 18], F32)
    gsc = sb("gsc", [128, 44], F32)
    usc = sb("usc", [128, 44], F32)
    id32 = sb("id32", [128, 128], F32)
    ones_kk = sb("ones_kk", [128, 1], BF16)
    ones_b32 = sb("ones_b32", [1, 128], F32)

    ps = [pt(f"ps{i}", [128, 512], F32) for i in range(8)]

    sems = {nm: es.enter_context(nc.semaphore(f"s_{nm}"))
            for nm in ("tM", "aM", "vM", "pM", "cC", "cG",
                       "cXT0", "cXT1", "cXF0", "cXF1", "cXF2", "cXF3",
                       "cW0", "cW1", "cD0", "cD1", "cD2", "oS0", "oS1",
                       "cK0", "cK1")}
    cnt = {nm: 0 for nm in sems}
    EV = {e: [] for e in ENGS}

    def ev(eng, fn, waits=(), inc=None, k=1):
        if inc is not None:
            cnt[inc] += k
        EV[eng].append((tuple((s, w) for s, w in waits if w and w > 0),
                        fn, inc, k))
        return cnt[inc] if inc else None

    def dma(eng, out, in_, waits=(), inc=None):
        return ev(eng, lambda e, o=out, i=in_: e.dma_start(out=o, in_=i),
                  waits=waits, inc=inc, k=16)

    def fs(i):
        return fscr[:, i * 512:(i + 1) * 512]

    ev("gpsimd", lambda e: e.memset(ones_kk[:, :], 1.0), inc="pM")
    ev("gpsimd", lambda e: e.memset(ones_b32[:, :], 1.0), inc="pM")
    w_ones = cnt["pM"]
    for dst, src in [(bqkv, d_bqkv), (gsc, d_gs), (usc, d_us), (id32, d_id)]:
        dma("sync", dst[:, :], src[:, :], inc="cC")
    w_cst = ("cC", cnt["cC"])

    xtm_i = [0]
    xf_i = [0]
    xtm_free = {}
    xf_free = {}
    tg_i = [0]
    LN_DVE_END = {}
    TR_LAST = {}
    ROW_FREE = {}
    RB_FREE = {}
    LN_RB = {}
    LN_BC = {}
    LN_NORM_END = {}

    def emit_ln_stats(src_tm, tm_row0):
        tg = tg_i[0]
        tg_i[0] += 1
        par = tg % 2
        Q0 = 0
        P0 = 32 + par * 48
        st = stat
        first_stats = True
        for ccj in range(4):
            ci = xtm_i[0]
            xtm_i[0] += 1
            sp = ci % 2
            tmslot = xtm[:, sp * 2048:(sp + 1) * 2048]
            wts = [w_cst]
            if (ci - 2) in xtm_free:
                wts.append(("vM", xtm_free[ci - 2]))
            xtsem = f"cXT{sp}"
            dma("sync", tmslot,
                src_tm[tm_row0 + ccj * 128: tm_row0 + (ccj + 1) * 128, :],
                waits=wts, inc=xtsem)
            ld = (xtsem, cnt[xtsem])
            stw = [ld]
            if first_stats:
                first_stats = False
                if (tg - 1) in LN_DVE_END:
                    stw.append(("vM", LN_DVE_END[tg - 1]))
            for j in range(4):
                ev("vector",
                   lambda e, o=st24[:, Q0 + ccj * 24 + 6 * j:
                                    Q0 + ccj * 24 + 6 * (j + 1)],
                   i=tmslot[:, j * 512:(j + 1) * 512]: e.bn_stats(o, i),
                   waits=stw if j == 0 else (),
                   inc="vM" if j == 3 else None)
            xtm_free[ci] = cnt["vM"]
        def vstep(fn, extra=()):
            pv = cnt["vM"]
            ev("vector", fn, waits=[("vM", pv)] + list(extra), inc="vM")
        means32 = st24[:, Q0 + 1:Q0 + 96:3]
        m2s32 = st24[:, Q0 + 2:Q0 + 96:3]
        means3 = means32.rearrange("p (c m) -> p c m", m=8)
        m2s3 = m2s32.rearrange("p (c m) -> p c m", m=8)
        msq = stat[:, 0:32]
        msq3 = msq.rearrange("p (c m) -> p c m", m=8)
        c_summ = st[:, P0 + 0:P0 + 4]
        c_sumM2 = st[:, P0 + 4:P0 + 8]
        c_summsq = st[:, P0 + 8:P0 + 12]
        c_m = st[:, P0 + 12:P0 + 16]
        c_ex2a = st[:, P0 + 16:P0 + 20]
        c_t5 = st[:, P0 + 20:P0 + 24]
        c_ex2 = st[:, P0 + 24:P0 + 28]
        c_msq2 = st[:, P0 + 28:P0 + 32]
        c_var = st[:, P0 + 32:P0 + 36]
        c_lnv = st[:, P0 + 36:P0 + 40]
        c_rstd = st[:, P0 + 40:P0 + 44]
        c_mr = st[:, P0 + 44:P0 + 48]
        vstep(lambda e, o=c_summ, i=means3: e.reduce_sum(o, i, axis=AX.X),
              extra=[("tM", TR_LAST[tg - 2])] if (tg - 2) in TR_LAST else ())
        vstep(lambda e, o=c_sumM2, i=m2s3: e.reduce_sum(o, i, axis=AX.X))
        vstep(lambda e, o=msq, a=means32, b=means32:
              e.tensor_tensor(o, a, b, ALU.mult))
        vstep(lambda e, o=c_summsq, i=msq3: e.reduce_sum(o, i, axis=AX.X))
        vstep(lambda e, o=c_m, i=c_summ:
              e.tensor_scalar(o, i, 0.125, None, ALU.mult))
        vstep(lambda e, o=c_ex2a, i=c_sumM2:
              e.tensor_scalar(o, i, 1.0 / 2048.0, None, ALU.mult))
        vstep(lambda e, o=c_t5, i=c_summsq:
              e.tensor_scalar(o, i, 0.125, None, ALU.mult))
        vstep(lambda e, o=c_ex2, a=c_ex2a, b=c_t5:
              e.tensor_tensor(o, a, b, ALU.add))
        vstep(lambda e, o=c_msq2, a=c_m, b=c_m:
              e.tensor_tensor(o, a, b, ALU.mult))
        vstep(lambda e, o=c_var, a=c_ex2, b=c_msq2:
              e.tensor_tensor(o, a, b, ALU.subtract))
        vstep(lambda e, o=c_lnv, i=c_var:
              e.tensor_scalar(o, i, EPS, None, ALU.add))
        var_done = cnt["vM"]
        lnw = [("vM", var_done)]
        if (tg - 2) in TR_LAST:
            lnw.append(("tM", TR_LAST[tg - 2]))
        ev("scalar", lambda e, o=c_lnv, i=c_lnv:
           e.activation(o, i, AF.Ln), waits=lnw, inc="aM")
        _lv = cnt["aM"]
        ev("scalar", lambda e, o=c_rstd, i=c_lnv:
           e.activation(o, i, AF.Exp, scale=-0.5),
           waits=[("aM", _lv)], inc="aM")
        rstd_done = cnt["aM"]
        vstep(lambda e, o=c_mr, a=c_m, b=c_rstd:
              e.tensor_tensor(o, a, b, ALU.mult),
              extra=[("aM", rstd_done)])
        mr_done = cnt["vM"]
        LN_DVE_END[tg] = mr_done
        for ccj in range(4):
            trw = []
            if ccj == 0:
                trw = [("aM", rstd_done), ("vM", mr_done), w_cst]
                if (tg - 2) in ROW_FREE:
                    trw.append(("vM", ROW_FREE[tg - 2]))
                if (tg - 2) in LN_BC:
                    trw.append(("tM", LN_BC[tg - 2]))
            ev("tensor", lambda e, o=ps[2][0:1, ccj * 128:(ccj + 1) * 128],
               i=c_rstd[:, ccj:ccj + 1]: e.transpose(o, i, id32[:, :]),
               waits=trw)
            ev("tensor", lambda e, o=ps[3][0:1, ccj * 128:(ccj + 1) * 128],
               i=c_mr[:, ccj:ccj + 1]: e.transpose(o, i, id32[:, :]),
               inc="tM" if ccj == 3 else None)
        TR_LAST[tg] = cnt["tM"]
        tr_last = cnt["tM"]
        rsl = rrow[0:1, par * 512:(par + 1) * 512]
        mrl = rrow[0:1, 1024 + par * 512:1024 + (par + 1) * 512]
        ev("vector", lambda e, o=rsl, i=ps[2][0:1, 0:512]: e.tensor_copy(o, i),
           waits=[("tM", tr_last)], inc="vM")
        _rv = cnt["vM"]
        ev("vector", lambda e, o=mrl, i=ps[3][0:1, 0:512]: e.tensor_copy(o, i),
           waits=[("vM", _rv)], inc="vM")
        ROW_FREE[tg] = cnt["vM"]
        row_done = cnt["vM"]
        bw = [("vM", row_done), ("pM", w_ones)]
        if (tg - 1) in RB_FREE:
            bw.append(("vM", RB_FREE[tg - 1]))
        ev("tensor", lambda e, o=ps[4][:, :], l=ones_b32[:, :], r=rsl:
           e.matmul(o, l, r, start=True, stop=True), waits=bw)
        ev("tensor", lambda e, o=ps[5][:, :], l=ones_b32[:, :], r=mrl:
           e.matmul(o, l, r, start=True, stop=True), inc="tM")
        LN_BC[tg] = cnt["tM"]
        bc_v = cnt["tM"]
        rb_s, mrb_s = fs(par), fs(2 + par)
        rbw = [("tM", bc_v)]
        if (tg - 2) in LN_NORM_END:
            rbw.append(("vM", LN_NORM_END[tg - 2]))
        ev("vector", lambda e, o=rb_s, i=ps[4][:, :]: e.tensor_copy(o, i),
           waits=rbw, inc="vM")
        _rc = cnt["vM"]
        ev("vector", lambda e, o=mrb_s, i=ps[5][:, :]: e.tensor_copy(o, i),
           waits=[("vM", _rc)], inc="vM")
        RB_FREE[tg] = cnt["vM"]
        rb_done = cnt["vM"]
        LN_RB[tg] = (rb_s, mrb_s, cnt["vM"])
        return tg

    def emit_ln_norm(tg, src_f, f_col0, dst_fn, norm_extra=(),
                     hc_range=None, xn_done=None):
        rb_s, mrb_s, rb_done = LN_RB[tg]
        if xn_done is None:
            xn_done = []
        for hc in (hc_range if hc_range is not None else range(HC)):
            fi = xf_i[0]
            xf_i[0] += 1
            fsl = xf[:, (fi % 4) * 512:(fi % 4 + 1) * 512]
            fw = [w_cst]
            if (fi - 4) in xf_free:
                fw.append(("vM", xf_free[fi - 4]))
            xfsem = f"cXF{fi % 4}"
            dma("sync", fsl, src_f[hc * 128:(hc + 1) * 128, f_col0:f_col0 + 512],
                waits=fw, inc=xfsem)
            fl = (xfsem, cnt[xfsem])
            t1 = fs(4 + hc % 2)
            t1w = [fl, ("vM", rb_done)]
            if hc >= 2:
                t1w.append(("vM", xn_done[hc - 2]))
            ev("vector", lambda e, o=t1, a=fsl, b=rb_s:
               e.tensor_tensor(o, a, b, ALU.mult),
               waits=t1w + (list(norm_extra) if hc == 0 else []), inc="vM")
            t1v_ = cnt["vM"]
            ev("vector", lambda e, o=dst_fn(hc), a=t1, b=mrb_s:
               e.tensor_tensor(o, a, b, ALU.subtract),
               waits=[("vM", t1v_)], inc="vM")
            xf_free[fi] = cnt["vM"]
            xn_done.append(cnt["vM"])
        if len(xn_done) == HC:
            LN_NORM_END[tg] = cnt["vM"]
        return xn_done

    xn_ready = {}

    def emit_norm_own(g):
        vals = emit_ln_norm(
            g, d_xf_o, g * 512,
            lambda hc, g=g: arena[:, A_XN + hc * 1024 + g * 512:
                                  A_XN + hc * 1024 + g * 512 + 512])
        for hc in range(HC):
            xn_ready[(hc, g)] = vals[hc]

    emit_ln_stats(d_xtm_o, 0)
    emit_ln_stats(d_xtm_o, 512)
    emit_norm_own(0)

    dma("gpsimd", wring[:, 0:2048], d_wq[16], inc="cK0")
    kv_k_ld = ("cK0", cnt["cK0"])
    dma("gpsimd", wring[:, 2048:4096], d_wq[17], inc="cK1")
    kv_v_ld = ("cK1", cnt["cK1"])

    wq_load = {}
    wq_free = {}
    q_done = {}
    k_done = {}
    v_done = {}
    kv_prev = {"K": None, "V": None}
    vstage_free = {}

    def emit_q_chunk(n):
        slot = wring[:, 4096 + (n % 2) * 2048: 4096 + (n % 2 + 1) * 2048]
        lw = []
        if (n - 2) in wq_free:
            lw.append(("tM", wq_free[n - 2]))
        wsem = f"cW{n % 2}"
        dma("gpsimd", slot, d_wq[n], waits=lw, inc=wsem)
        wq_load[n] = (wsem, cnt[wsem])
        for g in range(2):
            bank = ps[(n * 2 + g) % 2]
            xn_all_g = max(xn_ready[(hc, g)] for hc in range(HC))
            for hc in range(HC):
                if hc == 0:
                    wts = [wq_load[n], ("vM", xn_all_g)]
                    if (n - 1, g) in q_done:
                        wts.append(("aM", q_done[(n - 1, g)]))
                else:
                    wts = ()
                ev("tensor", lambda e, o=bank[:, :],
                   l=slot[:, hc * 128:(hc + 1) * 128],
                   r=arena[:, A_XN + hc * 1024 + g * 512:
                           A_XN + hc * 1024 + g * 512 + 512],
                   st=(hc == 0), sp=(hc == HC - 1):
                   e.matmul(o, l, r, start=st, stop=sp),
                   waits=wts, inc="tM" if hc == HC - 1 else None)
            mm_done = cnt["tM"]
            if g == 1:
                wq_free[n] = mm_done
            ev("scalar", lambda e,
               o=arena[:, A_Q + n * 1024 + g * 512:
                       A_Q + n * 1024 + g * 512 + 512],
               i=bank[:, :], b=bqkv[:, n:n + 1]:
               e.activation(o, i, AF.Identity, bias=b),
               waits=[("tM", mm_done), w_cst], inc="aM")
            q_done[(n, g)] = cnt["aM"]

    def emit_kv_tg(tg, xn_vals, xn_fn):
        for nn, nm in ((16, "K"), (17, "V")):
            bank = ps[6] if nm == "K" else ps[7]
            xn_all = max(xn_vals)
            for hc in range(HC):
                wts = ()
                if hc == 0:
                    wts = [("vM", xn_all),
                           kv_k_ld if nm == "K" else kv_v_ld]
                    if kv_prev[nm] is not None:
                        wts.append(kv_prev[nm])
                    if tg == 2:
                        if nm == "K":
                            wts.append(("aM", ex_done[NB_A - 1]))
                        else:
                            wts.append(("vM", rbA_done[13]))
                    elif tg == 3:
                        if nm == "K":
                            wts.append(("aM", ex_done[NB_A + 7]))
                        else:
                            wts.append(("vM", rbA_done[14]))
                ev("tensor", lambda e, o=bank[:, :],
                   l=wring[:, (nn - 16) * 2048 + hc * 128:
                           (nn - 16) * 2048 + (hc + 1) * 128],
                   r=xn_fn(hc), st=(hc == 0), sp=(hc == HC - 1):
                   e.matmul(o, l, r, start=st, stop=sp),
                   waits=wts, inc="tM" if hc == HC - 1 else None)
            mm_done = cnt["tM"]
            if nm == "K":
                ev("scalar", lambda e,
                   o=arena[:, A_K + tg * 512:A_K + (tg + 1) * 512],
                   i=bank[:, :], b=bqkv[:, 16:17]:
                   e.activation(o, i, AF.Identity, bias=b),
                   waits=[("tM", mm_done), w_cst], inc="aM")
                k_done[tg] = cnt["aM"]
                kv_prev["K"] = ("aM", cnt["aM"])
            else:
                vst = fs(6)
                vw = [("tM", mm_done), w_cst]
                if (tg - 1) in vstage_free:
                    vw.append(("tM", vstage_free[tg - 1]))
                ev("scalar", lambda e, o=vst, i=bank[:, :], b=bqkv[:, 17:18]:
                   e.activation(o, i, AF.Identity, bias=b),
                   waits=vw, inc="aM")
                vev = cnt["aM"]
                for j in range(4):
                    ev("tensor", lambda e, o=ps[7][:, j * 128:(j + 1) * 128],
                       i=vst[:, j * 128:(j + 1) * 128]:
                       e.transpose(o, i, id32[:, :]),
                       waits=[("aM", vev)] if j == 0 else (),
                       inc="tM" if j == 3 else None)
                vstage_free[tg] = cnt["tM"]
                tr_done = cnt["tM"]
                ev("vector", lambda e,
                   o=arena[:, A_V + tg * 512:A_V + (tg + 1) * 512],
                   i=ps[7][:, 0:512]: e.tensor_copy(o, i),
                   waits=[("tM", tr_done)], inc="vM")
                v_done[tg] = cnt["vM"]
                kv_prev["V"] = ("vM", cnt["vM"])

    def emit_key_norm_kv(ktg):
        tgid = 2 + ktg
        base = 0 if ktg == 0 else 8192
        extra = [("tM", vstage_free[ktg - 1])] if ktg >= 2 else []
        xn_fn = lambda hc, base=base: arena[:, A_AT + base + hc * 512:
                                            A_AT + base + (hc + 1) * 512]
        vals = emit_ln_norm(tgid, d_xf_g, ktg * 512, xn_fn, norm_extra=extra)
        emit_kv_tg(ktg, vals, xn_fn)

    emit_ln_stats(d_xtm_g, 0)
    emit_norm_own(1)
    for n in range(0, 8):
        emit_q_chunk(n)
    emit_ln_stats(d_xtm_g, 512)
    emit_key_norm_kv(0)
    for n in range(8, 12):
        emit_q_chunk(n)
    dma("gpsimd", arena[:, A_MSK:A_MSK + 8192], d_msk[:, :], inc="cG")
    dma("gpsimd", arena[:, A_BC:A_BC + 2048], d_bproj[:, :], inc="cG")
    dma("gpsimd", arena[:, A_BC + 2048:A_BC + 4096], d_dsb[:, :], inc="cG")
    w_msk = w_bproj = w_dsb = ("cG", cnt["cG"])

    emit_ln_stats(d_xtm_g, 1024)
    emit_key_norm_kv(1)
    for n in range(12, 16):
        emit_q_chunk(n)
    emit_ln_stats(d_xtm_g, 1536)

    groups = [(h, 0) for h in range(NH)] + [(h, 1) for h in range(NH)]
    blocks = []
    for hi, (h, g) in enumerate(groups):
        for kc in range(NKC[g]):
            blocks.append((hi, h, g, kc))
    NB = len(blocks)

    sc_done = [0] * NB
    ex_done = [0] * NB
    pd_done = [0] * NB
    pd_sem = [None] * NB
    pv_done = [0] * NB
    rc_done = {}
    bcA_done = {}
    rbA_done = {}
    at_done = {}
    grp_sums_stop = {}
    grp_pv_stop = {}

    q_last = q_done[(15, 1)]
    rowfree_last = ROW_FREE[tg_i[0] - 1]
    rbfree_last = RB_FREE[tg_i[0] - 1]

    def blk_masked(g, kc):
        return g == 0 or kc >= 8

    def mask_slot(g, kc):
        return g * 8 + (kc if g == 0 else kc - 8)

    SC_BANKS = (ps[0], ps[1], ps[6])

    def emit_scores(t):
        hi, h, g, kc = blocks[t]
        bank = SC_BANKS[t % 3]
        wts = []
        if kc == 0:
            wts.append(("aM", q_done[(h, g)]))
        if kc % 4 == 0:
            wts.append(("aM", k_done[kc // 4]))
        if t >= 3:
            wts.append(("aM", ex_done[t - 3]))
        elif t < 2:
            wts.append(("aM", q_last))
        if NB_A <= t < NB_A + 3:
            wts.append(("aM", k_done[2]))
            wts.append(("vM", v_done[2]))
        if NB_A + 8 <= t < NB_A + 11:
            wts.append(("aM", k_done[3]))
            wts.append(("vM", v_done[3]))
        ev("tensor", lambda e, o=bank[:, :],
           l=arena[:, A_K + kc * 128:A_K + (kc + 1) * 128],
           r=arena[:, A_Q + h * 1024 + g * 512:A_Q + h * 1024 + (g + 1) * 512]:
           e.matmul(o, l, r, start=True, stop=True),
           waits=wts, inc="tM")
        sc_done[t] = cnt["tM"]

    def emit_exp(t):
        hi, h, g, kc = blocks[t]
        bank = SC_BANKS[t % 3]
        slot = arena[:, A_PR + (t % R_P) * 512:A_PR + (t % R_P + 1) * 512]
        wts = [("tM", sc_done[t])]
        if t >= R_P:
            wts.append(("tM", pv_done[t - R_P]))
        ev("scalar", lambda e, o=slot, i=bank[:, :]:
           e.activation(o, i, AF.Exp, scale=SCALE),
           waits=wts, inc="aM")
        ex_done[t] = cnt["aM"]
        if blk_masked(g, kc):
            ms = mask_slot(g, kc)
            ev("vector", lambda e, o=slot, a=slot,
               b=arena[:, A_MSK + ms * 512:A_MSK + (ms + 1) * 512]:
               e.tensor_tensor(o, a, b, ALU.mult),
               waits=[("aM", ex_done[t]), w_msk], inc="vM")
            pd_done[t], pd_sem[t] = cnt["vM"], "vM"
        else:
            pd_done[t], pd_sem[t] = ex_done[t], "aM"

    def emit_sums_pv(t):
        hi, h, g, kc = blocks[t]
        slot = arena[:, A_PR + (t % R_P) * 512:A_PR + (t % R_P + 1) * 512]
        sums_bank = ps[4 + hi % 2]
        pv_bank = ps[2 + hi % 2]
        first, last = kc == 0, kc == NKC[g] - 1
        wts = [(pd_sem[t], pd_done[t])]
        if first:
            wts.append(("pM", w_ones))
            if (hi - 2) in rc_done:
                wts.append(rc_done[hi - 2])
            elif hi < 2:
                wts.append(("vM", rbfree_last))
        ev("tensor", lambda e, o=sums_bank[0:1, :], l=ones_kk[:, :], r=slot,
           st=first, sp=last: e.matmul(o, l, r, start=st, stop=sp),
           waits=wts, inc="tM" if last else None)
        if last:
            grp_sums_stop[hi] = cnt["tM"]
        pw = []
        if kc % 4 == 0:
            pw.append(("vM", v_done[kc // 4]))
        if first:
            if (hi - 2) in at_done:
                pw.append(("vM", at_done[hi - 2]))
            elif hi < 2:
                pw.append(("vM", rowfree_last))
        ev("tensor", lambda e, o=pv_bank[:, :],
           l=arena[:, A_V + kc * 128:A_V + (kc + 1) * 128],
           r=slot, st=first, sp=last:
           e.matmul(o, l, r, start=st, stop=sp),
           waits=pw, inc="tM")
        pv_done[t] = cnt["tM"]
        if last:
            grp_pv_stop[hi] = cnt["tM"]

    def emit_tail_dve(hi):
        par = hi % 2
        rsl = outt[0:1, par * 512:(par + 1) * 512]
        lsl = outt[32:33, par * 512:(par + 1) * 512]
        ev("scalar", lambda e, o=lsl, i=ps[4 + par][0:1, :]:
           e.activation(o, i, AF.Ln),
           waits=[("tM", grp_sums_stop[hi])], inc="aM")
        _lv = cnt["aM"]
        ev("scalar", lambda e, o=rsl, i=lsl:
           e.activation(o, i, AF.Exp, scale=-1.0),
           waits=[("aM", _lv)], inc="aM")
        rc_done[hi] = ("aM", cnt["aM"])

    def emit_tail_bcast(hi):
        par = hi % 2
        rsl = outt[0:1, par * 512:(par + 1) * 512]
        wts = [rc_done[hi]]
        if (hi - 1) in rbA_done:
            wts.append(("vM", rbA_done[hi - 1]))
        if hi == 0:
            wts.append(("tM", vstage_free[1]))
        if hi == 14:
            wts.append(("tM", vstage_free[2]))
            wts.append(("vM", v_done[2]))
        if hi in (15, 16):
            wts.append(("tM", vstage_free[3]))
            wts.append(("vM", v_done[3]))
        ev("tensor", lambda e, o=ps[7][:, :], l=ones_b32[:, :], r=rsl:
           e.matmul(o, l, r, start=True, stop=True),
           waits=wts, inc="tM")
        bcA_done[hi] = cnt["tM"]

    def emit_tail_evac(hi):
        par = hi % 2
        h, g = groups[hi]
        rbs = xtm[:, par * 512:(par + 1) * 512]
        rcw = [("tM", bcA_done[hi])]
        if (hi - 2) in at_done:
            rcw.append(("vM", at_done[hi - 2]))
        ev("vector", lambda e, o=rbs, i=ps[7][:, :]: e.tensor_copy(o, i),
           waits=rcw, inc="vM")
        rbA_done[hi] = cnt["vM"]
        ev("vector", lambda e,
           o=arena[:, A_AT + g * 8192 + h * 512:
                   A_AT + g * 8192 + (h + 1) * 512],
           a=ps[2 + par][:, :], b=rbs:
           e.tensor_tensor(o, a, b, ALU.mult),
           waits=[("tM", grp_pv_stop[hi]), ("vM", rbA_done[hi])], inc="vM")
        at_done[hi] = cnt["vM"]

    NB_A = sum(NKC[0] for _ in range(NH))
    key_vals = {}
    key_xfn = {}

    def emit_key_norm_only(ktg):
        tgid = 2 + ktg
        base = 0 if ktg == 0 else 8192
        extra = [("tM", vstage_free[ktg - 1])] if ktg >= 2 else []
        xn_fn = lambda hc, base=base: arena[:, A_AT + base + hc * 512:
                                            A_AT + base + (hc + 1) * 512]
        key_vals[ktg] = emit_ln_norm(tgid, d_xf_g, ktg * 512, xn_fn,
                                     norm_extra=extra)
        key_xfn[ktg] = xn_fn

    def emit_key_norm_piece(ktg, piece):
        tgid = 2 + ktg
        base = 0 if ktg == 0 else 8192
        extra = ([("tM", vstage_free[ktg - 1])]
                 if (ktg >= 2 and piece == 0) else [])
        xn_fn = lambda hc, base=base: arena[:, A_AT + base + hc * 512:
                                            A_AT + base + (hc + 1) * 512]
        key_xfn[ktg] = xn_fn
        acc = key_vals.setdefault(ktg, [])
        emit_ln_norm(tgid, d_xf_g, ktg * 512, xn_fn, norm_extra=extra,
                     hc_range=range(piece * 4, (piece + 1) * 4), xn_done=acc)

    pend = None
    for t in range(NB):
        hi, h, g, kc = blocks[t]
        if NB_A - 40 <= t < NB_A - 40 + 32 and (t - (NB_A - 40)) % 8 == 0:
            emit_key_norm_piece(2, (t - (NB_A - 40)) // 8)
        if t == NB_A:
            emit_kv_tg(2, key_vals[2], key_xfn[2])
        if NB_A <= t < NB_A + 8 and (t - NB_A) % 2 == 0:
            emit_key_norm_piece(3, (t - NB_A) // 2)
        if t == NB_A + 8:
            emit_kv_tg(3, key_vals[3], key_xfn[3])
        emit_scores(t)
        if kc == 1 and pend is not None:
            emit_tail_bcast(pend)
            emit_tail_evac(pend)
            pend = None
        emit_exp(t)
        if t >= 2:
            emit_sums_pv(t - 2)
            phi, _, pg, pkc = blocks[t - 2]
            if pkc == NKC[pg] - 1:
                emit_tail_dve(phi)
                pend = phi
    for tt in (NB - 2, NB - 1):
        emit_sums_pv(tt)
        phi, _, pg, pkc = blocks[tt]
        if pkc == NKC[pg] - 1:
            emit_tail_dve(phi)
            if pend is not None and pend != phi:
                emit_tail_bcast(pend)
                emit_tail_evac(pend)
            pend = phi
    if pend is not None and pend != len(groups) - 1:
        emit_tail_bcast(pend)
        emit_tail_evac(pend)
    emit_tail_bcast(len(groups) - 1)
    emit_tail_evac(len(groups) - 1)

    wp_load = {}
    wp_free = {}
    s_done = {}
    pj_free = {}
    attn_all = max(at_done[len(groups) - 1], rbA_done[len(groups) - 1])
    exp_all = max(ex_done[NB - 1], rc_done[len(groups) - 1][1])

    for sw in range(4):
        fgh, qq = sw // 2, sw % 2
        for a in range(16):
            ap_ = a // 2
            slot = wring[:, (ap_ % 4) * 2048:(ap_ % 4 + 1) * 2048]
            if a % 2 == 0:
                lw = []
                prev = (sw, a - 7) if a >= 8 else \
                       ((sw - 1, 9 + a) if sw > 0 else None)
                if prev in wp_free:
                    lw.append(("tM", wp_free[prev]))
                if sw == 0 and a < 8:
                    lw.append(("tM", vstage_free[3]))
                wsem = ("cW0", "cW1", "cK0", "cK1")[ap_ % 4]
                dma("gpsimd",
                    slot.rearrange("p (i f) -> p i f", f=1024),
                    d_wp[a:a + 2][:, :, fgh * 1024:(fgh + 1) * 1024]
                    .transpose([1, 0, 2]),
                    waits=lw, inc=wsem)
                wp_load[(sw, a)] = (wsem, cnt[wsem])
                wp_load[(sw, a + 1)] = (wsem, cnt[wsem])
            first_mm = True
            for qc_l in range(4):
                qcg = qq * 4 + qc_l
                g = qcg // 4
                for fgi in range(2):
                    bank = ps[qc_l * 2 + fgi]
                    wts = ()
                    if first_mm:
                        first_mm = False
                        wts = [wp_load[(sw, a)],
                               ("vM", max(at_done[groups.index((a, 0))], at_done[groups.index((a, 1))]))]
                        if a == 0:
                            if sw >= 1:
                                wts.append(("vM", max(
                                    pj_free[(sw - 1, b)] for b in range(8))))
                            else:
                                wts.append(("aM", exp_all))
                                wts.append(("vM", attn_all))
                    ev("tensor", lambda e, o=bank[:, :],
                       l=arena[:, A_AT + g * 8192 + a * 512 + (qcg % 4) * 128:
                               A_AT + g * 8192 + a * 512 + (qcg % 4 + 1) * 128],
                       r=slot[:, (a % 2) * 1024 + fgi * 512:
                              (a % 2) * 1024 + (fgi + 1) * 512],
                       st=(a == 0), sp=(a == 15):
                       e.matmul(o, l, r, start=st, stop=sp),
                       waits=wts,
                       inc="tM" if (qc_l == 3 and fgi == 1) else None)
            wp_free[(sw, a)] = cnt["tM"]
        sweep_done = cnt["tM"]
        for qc_l in range(4):
            qcg = qq * 4 + qc_l
            for fgi in range(2):
                fg = fgh * 2 + fgi
                bank = ps[qc_l * 2 + fgi]
                ev("vector", lambda e,
                   o=arena[:, A_Q + qcg * 2048 + fg * 512:
                           A_Q + qcg * 2048 + (fg + 1) * 512],
                   a=bank[:, :],
                   b=arena[:, A_BC + fg * 512:A_BC + (fg + 1) * 512]:
                   e.tensor_tensor(o, a, b, ALU.add),
                   waits=[("tM", sweep_done), w_bproj], inc="vM")
                s_done[(qcg, fg)] = cnt["vM"]
                pj_free[(sw, qc_l * 2 + fgi)] = cnt["vM"]
    pj_all = max(pj_free.values())

    gu_load = {}
    gu_free = {}
    hdn_done = {}
    attn_dve_end = attn_all
    dn_load = {}
    dn_free = {}
    dn_slot_prev = {}
    epi_t1 = {}
    epi_gv = {}
    epi_t2v = {}
    epi_outv = {}
    gv_last = {}
    out_last = {}
    xtile_free = {}
    store_v = {}
    out_i = [0]
    last_down = [0]

    for g in range(2):
        sg_free = {}
        us_free = {}
        for i in range(IC):
            slot = wring[:, (i % 2) * 4096:(i % 2 + 1) * 4096]
            key = (g, i)
            ld2 = []
            for half in range(2):
                prev = (g, i - 2) if i >= 2 else \
                       ((g - 1, IC - 2 + i) if g > 0 else None)
                lw = []
                if half == 0:
                    if prev in gu_free:
                        lw.append(("tM", gu_free[prev]))
                    if g == 0 and i < 2:
                        lw.append(("tM", wp_free[(3, 15)]))
                wsem = f"cW{(2 * i + half) % 4}" if False else f"cW{i % 2}"
                hsem = ("cW0", "cW1", "cK0", "cK1")[(2 * i + half) % 4]
                dma("gpsimd", slot[:, half * 2048:(half + 1) * 2048],
                    d_gu[i][:, half * 2048:(half + 1) * 2048],
                    waits=lw, inc=hsem)
                ld2.append((hsem, cnt[hsem]))
            gu_load[key] = ld2
            gu_load2 = ld2
            gb = ps[0 + (i % 2) * 2]
            ub = ps[1 + (i % 2) * 2]
            for hc in range(HC):
                wts = ()
                if hc == 0:
                    wts = [gu_load[key][0]]
                    if prev in sg_free:
                        wts.append(("aM", sg_free[prev]))
                    if g == 0 and i < 2:
                        wts.append(("vM", pj_all))
                    if g == 1 and i == 0:
                        wts.append(("tM", last_down[0]))
                        wts.append(("vM", epi_t1[(0, 3, 3)]))
                ev("tensor", lambda e, o=gb[:, :],
                   l=slot[:, hc * 128:(hc + 1) * 128],
                   r=arena[:, A_XN + hc * 1024 + g * 512:
                           A_XN + hc * 1024 + g * 512 + 512],
                   st=(hc == 0), sp=(hc == HC - 1):
                   e.matmul(o, l, r, start=st, stop=sp),
                   waits=wts, inc="tM" if hc == HC - 1 else None)
            g_stop = cnt["tM"]
            for hc in range(HC):
                wts = ()
                if hc == 0:
                    wts = [gu_load[key][1]]
                    if prev in us_free:
                        wts.append(("vM", us_free[prev]))
                    if g == 0 and i < 2:
                        wts.append(("vM", pj_all))
                ev("tensor", lambda e, o=ub[:, :],
                   l=slot[:, 2048 + hc * 128:2048 + (hc + 1) * 128],
                   r=arena[:, A_XN + hc * 1024 + g * 512:
                           A_XN + hc * 1024 + g * 512 + 512],
                   st=(hc == 0), sp=(hc == HC - 1):
                   e.matmul(o, l, r, start=st, stop=sp),
                   waits=wts, inc="tM" if hc == HC - 1 else None)
            u_stop = cnt["tM"]
            gu_free[key] = cnt["tM"]
            sgs = fs(4 + i % 2)
            uss = fs(6 + i % 2)
            sgw = [("tM", g_stop), w_cst]
            if prev in hdn_done:
                sgw.append(("vM", hdn_done[prev]))
            ev("scalar", lambda e, o=sgs, i_=gb[:, :], sc=gsc[:, i:i + 1]:
               e.activation(o, i_, AF.Silu, scale=sc),
               waits=sgw, inc="aM")
            sg_free[key] = cnt["aM"]
            sg_v = cnt["aM"]
            usw = [("tM", u_stop), w_cst]
            if prev in hdn_done:
                usw.append(("vM", hdn_done[prev]))
            ev("vector", lambda e, o=uss, i_=ub[:, :], sc=usc[:, i:i + 1]:
               e.tensor_scalar(o, i_, sc, None, ALU.mult),
               waits=usw, inc="vM")
            us_free[key] = cnt["vM"]
            hw = [("aM", sg_v), ("vM", us_free[key])]
            if g == 1 and i == 0:
                hw.append(("tM", last_down[0]))
            if g == 0 and i == 32:
                hw.append(("vM", attn_dve_end))
            ev("vector", lambda e,
               o=arena[:, A_HDN + i * 512:A_HDN + (i + 1) * 512],
               a=sgs, b=uss: e.tensor_tensor(o, a, b, ALU.mult),
               waits=hw, inc="vM")
            hdn_done[(g, i)] = cnt["vM"]

        for fgp in range(2):
            def emit_dn_load(i, fgp=fgp, g=g):
                key = (g, fgp, i)
                slot = i % 3
                pkey = dn_slot_prev.get(slot)
                dn_slot_prev[slot] = key
                lw = []
                if pkey is not None:
                    assert pkey in dn_free, f"dn pacing missing {pkey}"
                    lw.append(("tM", dn_free[pkey]))
                dsem = f"cD{slot}"
                dma("gpsimd",
                    wd[:, slot * 1024:(slot + 1) * 1024],
                    d_dn[fgp][i],
                    waits=lw, inc=dsem)
                dn_load[key] = (dsem, cnt[dsem])
            for i in range(3):
                emit_dn_load(i)
            def emit_dn_mm(i, qc_l, fgi, fgp=fgp, g=g):
                bank = ps[qc_l * 2 + fgi]
                wts = []
                if qc_l == 2 and fgi == 0:
                    wts.append(("vM", hdn_done[(g, i)]))
                    wts.append(dn_load[(g, fgp, i)])
                if i == 0 and fgi == 0 and qc_l == 0:
                    wts.append(("aM", sg_free[(g, IC - 1)]))
                    wts.append(("vM", us_free[(g, IC - 1)]))
                if i == 0:
                    fg = fgp * 2 + fgi
                    pk = (g, fg - 2, qc_l) if fgp > 0 else \
                        ((g - 1, fg + 2, qc_l) if g > 0 else None)
                    if pk in epi_t1:
                        wts.append(("vM", epi_t1[pk]))
                    elif g == 0 and fgp == 0 and qc_l == 0 and fgi == 0:
                        wts.append(("vM", pj_all))
                ev("tensor", lambda e, o=bank[:, :],
                   l=arena[:, A_HDN + i * 512 + qc_l * 128:
                           A_HDN + i * 512 + (qc_l + 1) * 128],
                   r=wd[:, (i % 3) * 1024 + fgi * 512:
                        (i % 3) * 1024 + (fgi + 1) * 512],
                   st=(i == 0), sp=(i == IC - 1):
                   e.matmul(o, l, r, start=st, stop=sp),
                   waits=wts, inc="tM")

            DLAG = 2
            for i in range(IC + DLAG):
                if i < IC:
                    for qc_l in (2, 3):
                        for fgi in range(2):
                            emit_dn_mm(i, qc_l, fgi)
                if i >= DLAG:
                    for qc_l in (0, 1):
                        for fgi in range(2):
                            emit_dn_mm(i - DLAG, qc_l, fgi)
                    dn_free[(g, fgp, i - DLAG)] = cnt["tM"]
                    if i - DLAG + 3 < IC:
                        emit_dn_load(i - DLAG + 3)
            dn_stop = cnt["tM"]
            last_down[0] = dn_stop
            for fgi in range(2):
                fg = fgp * 2 + fgi
                xls = {}
                for qc_l in range(4):
                    fsl = xf[:, qc_l * 512:(qc_l + 1) * 512]
                    pk = (g, fg - 1, qc_l) if fg > 0 else \
                        ((g - 1, 3, qc_l) if g > 0 else None)
                    xw = []
                    if pk in xtile_free:
                        xw.append(("vM", xtile_free[pk]))
                    elif g == 0 and fg == 0:
                        xw.append(("vM", xf_free[xf_i[0] - 1]))
                    xfsem = f"cXF{qc_l}"
                    dma("sync", fsl,
                        d_xtm_o[g * 512 + qc_l * 128:g * 512 + (qc_l + 1) * 128,
                                fg * 512:(fg + 1) * 512],
                        waits=xw, inc=xfsem)
                    xls[qc_l] = (xfsem, cnt[xfsem])
                for qc_l in (2, 3, 0, 1):
                    bank = ps[qc_l * 2 + fgi]
                    t1 = fs(4 + qc_l % 2)
                    t2 = fs(6 + qc_l % 2)
                    oi = out_i[0]
                    out_i[0] += 1
                    osl = outt[:, (oi % 2) * 512:(oi % 2 + 1) * 512]
                    t1w = [("tM", dn_stop), w_dsb]
                    if qc_l % 2 in gv_last:
                        t1w.append(("aM", gv_last[qc_l % 2]))
                    ev("vector", lambda e, o=t1, a=bank[:, :],
                       b=arena[:, A_BC + 2048 + fg * 512:
                               A_BC + 2048 + (fg + 1) * 512]:
                       e.tensor_tensor(o, a, b, ALU.mult),
                       waits=t1w, inc="vM")
                    t1v = cnt["vM"]
                    epi_t1[(g, fg, qc_l)] = t1v
                    gsl = arena[:, A_PR + qc_l * 512:A_PR + (qc_l + 1) * 512]
                    gvw = [("vM", t1v)]
                    pk3 = (g, fg - 1, qc_l) if fg > 0 else \
                        ((g - 1, 3, qc_l) if g > 0 else None)
                    if pk3 in epi_t2v:
                        gvw.append(("vM", epi_t2v[pk3]))
                    ev("scalar", lambda e, o=gsl, i_=t1:
                       e.activation(o, i_, AF.Silu),
                       waits=gvw, inc="aM")
                    gv = cnt["aM"]
                    epi_gv[(g, fg, qc_l)] = gv
                    gv_last[qc_l % 2] = gv
                    qcg = g * 4 + qc_l
                    ev("vector", lambda e, o=t2, a=gsl,
                       b=arena[:, A_Q + qcg * 2048 + fg * 512:
                               A_Q + qcg * 2048 + (fg + 1) * 512]:
                       e.tensor_tensor(o, a, b, ALU.mult),
                       waits=[("aM", gv), ("vM", s_done[(qcg, fg)])] +
                             ([("vM", out_last[qc_l % 2])]
                              if qc_l % 2 in out_last else []),
                       inc="vM")
                    epi_t2v[(g, fg, qc_l)] = cnt["vM"]
                    ow = [xls[qc_l], ("vM", cnt["vM"])]
                    if oi >= 2:
                        ow.append(store_v[oi - 2])
                    ev("vector", lambda e, o=osl, a=t2,
                       b=xf[:, qc_l * 512:(qc_l + 1) * 512]:
                       e.tensor_tensor(o, a, b, ALU.add),
                       waits=ow, inc="vM")
                    ov = cnt["vM"]
                    epi_outv[(g, fg, qc_l)] = ov
                    out_last[qc_l % 2] = ov
                    xtile_free[(g, fg, qc_l)] = ov
                    osem = f"oS{oi % 2}"
                    dma("sync",
                        d_out[g * 512 + qc_l * 128:g * 512 + (qc_l + 1) * 128,
                              fg * 512:(fg + 1) * 512],
                        osl, waits=[("vM", ov)], inc=osem)
                    store_v[oi] = (osem, cnt[osem])

    if debug:
        allw = [("vM", cnt["vM"]), ("aM", cnt["aM"]), ("tM", cnt["tM"])]
        for nm, off, width in [("dbg_xn", A_XN, 16384), ("dbg_k", A_K, 2048),
                               ("dbg_v", A_V, 2048), ("dbg_at", A_AT, 16384),
                               ("dbg_s", A_Q, 16384), ("dbg_q", A_Q, 16384),
                               ("dbg_hd", A_HDN, 22528)]:
            dma("sync", dbg_t[nm][:, :], arena[:, off:off + width],
                waits=allw, inc="oS0")
            allw = []

    ev("sync", lambda e: e.wait_ge(sems["oS0"], cnt["oS0"]))
    ev("sync", lambda e: e.wait_ge(sems["oS1"], cnt["oS1"]))

    with es:
        with nc.Block() as blk:
            def mk_body(eng_nm):
                evs = EV[eng_nm]
                def body(e):
                    for waits, fn, incn, k in evs:
                        for snm, w in waits:
                            e.wait_ge(sems[snm], w)
                        instr = fn(e)
                        if incn is not None:
                            instr.then_inc(sems[incn], k)
                return body
            blk.gpsimd(mk_body("gpsimd"))
            blk.sync(mk_body("sync"))
            blk.tensor(mk_body("tensor"))
            blk.scalar(mk_body("scalar"))
            blk.vector(mk_body("vector"))
    return nc


def _prep_shared(W_qkv, b_qkv, W_proj, b_proj, gate_w, up_w, down_w,
                 gate_s, up_s, down_s):
    f32 = np.float32
    W_qkv = np.ascontiguousarray(np.asarray(W_qkv, f32))
    wq_t = np.ascontiguousarray(
        W_qkv.reshape(16, 128, 18, 128).transpose(2, 1, 0, 3)).reshape(
            18, 128, 2048)
    bqkv_t = np.ascontiguousarray(np.asarray(b_qkv, f32).reshape(18, 128).T)
    wp_t = np.ascontiguousarray(np.asarray(W_proj, f32)).reshape(16, 128, 2048)
    bproj_b = np.ascontiguousarray(
        np.broadcast_to(np.asarray(b_proj, f32), (128, 2048)))
    gt = np.ascontiguousarray(
        np.asarray(gate_w, f32).reshape(44, 128, 16, 128).transpose(0, 3, 2, 1)
    ).reshape(44, 128, 2048)
    ut = np.ascontiguousarray(
        np.asarray(up_w, f32).reshape(44, 128, 16, 128).transpose(0, 3, 2, 1)
    ).reshape(44, 128, 2048)
    gu_t = np.ascontiguousarray(np.concatenate([gt, ut], axis=2))
    gs_t = np.ascontiguousarray(np.asarray(gate_s, f32).reshape(44, 128).T)
    us_t = np.ascontiguousarray(np.asarray(up_s, f32).reshape(44, 128).T)
    dn = np.ascontiguousarray(np.asarray(down_w, f32).T).reshape(44, 128, 2048)
    dn_t = np.ascontiguousarray(
        dn.reshape(44, 128, 2, 1024).transpose(2, 0, 1, 3))
    dsb_b = np.ascontiguousarray(
        np.broadcast_to(np.asarray(down_s, f32), (128, 2048)))
    ident = np.eye(128, dtype=f32)
    return dict(wq_t=wq_t, bqkv_t=bqkv_t, wp_t=wp_t, bproj_b=bproj_b,
                gu_t=gu_t, gs_t=gs_t, us_t=us_t, dn_t=dn_t, dsb_b=dsb_b,
                ident=ident)


def _masks_for(sub):
    qstarts = (0, 1536) if sub == 0 else (512, 1024)
    m = np.zeros((128, 16 * 512), dtype=np.float32)
    kk = np.arange(128)[:, None]
    q = np.arange(512)[None, :]
    for g in range(2):
        for j in range(8):
            kc = j if g == 0 else 8 + j
            valid = (kc * 128 + kk) <= (qstarts[g] + q)
            m[:, (g * 8 + j) * 512:(g * 8 + j + 1) * 512] = \
                valid.astype(np.float32)
    return m


def _in_maps(x, shared):
    masks = [_masks_for(0), _masks_for(1)]
    in_maps, own_all = [], []
    for c in range(8):
        b, sub = c // 2, c % 2
        qs = (0, 1536) if sub == 0 else (512, 1024)
        own = np.r_[qs[0]:qs[0] + 512, qs[1]:qs[1] + 512]
        own_all.append(own)
        xb = x[b]
        im = dict(shared)
        im["xf_g"] = np.ascontiguousarray(xb.T)
        im["xtm_g"] = np.ascontiguousarray(xb)
        im["xf_o"] = np.ascontiguousarray(xb[own].T)
        im["xtm_o"] = np.ascontiguousarray(xb[own])
        im["masks"] = masks[sub]
        in_maps.append(im)
    return in_maps, own_all


def kernel(L0_attn_x, L0_attn_W_qkv, L0_attn_b_qkv, L0_attn_W_proj,
           L0_attn_b_proj, L0_ffn_gate_w, L0_ffn_up_w, L0_ffn_down_w,
           L0_ffn_gate_s, L0_ffn_up_s, L0_ffn_down_s):
    f32 = np.float32
    x = np.ascontiguousarray(np.asarray(L0_attn_x, f32))
    shared = _prep_shared(L0_attn_W_qkv, L0_attn_b_qkv, L0_attn_W_proj,
                          L0_attn_b_proj, L0_ffn_gate_w, L0_ffn_up_w,
                          L0_ffn_down_w, L0_ffn_gate_s, L0_ffn_up_s,
                          L0_ffn_down_s)
    nc = build()
    in_maps, own_all = _in_maps(x, shared)
    res = run_bass_kernel_spmd(nc, in_maps, core_ids=list(range(8)))
    out = np.empty((B, S, H), dtype=f32)
    for c in range(8):
        out[c // 2][own_all[c]] = np.asarray(res.results[c]["out"])
    return out


# version 28
# speedup vs baseline: 1.3907x; 1.3907x over previous
import numpy as np
from contextlib import ExitStack

import concourse.bass as bass
import concourse.mybir as mybir
from concourse.bass_utils import run_bass_kernel_spmd

F32 = mybir.dt.float32
BF16 = mybir.dt.bfloat16
AF = mybir.ActivationFunctionType
ALU = mybir.AluOpType
AX = mybir.AxisListType

B, S, H, NH, HD, INTER = 4, 2048, 2048, 16, 128, 5632
SCALE = float(HD) ** -0.5
EPS = 1e-5
HC = H // 128
IC = INTER // 128
NKC = (8, 16)
R_P = 4

A_XN = 0
A_Q = 16384
A_AT = 32768
A_MSK = 49152
A_HDN = 32768
A_K = 57344
A_V = 59392
A_PR = 61440
A_BC = 63488
ARENA_E = 67584

ENGS = ("sync", "gpsimd", "tensor", "scalar", "vector")



def build(debug=False, dbg_tap=None):
    nc = bass.Bass("TRN2", target_bir_lowering=False)

    d_xf_g = nc.dram_tensor("xf_g", [H, S], F32, kind="ExternalInput").ap()
    d_xtm_g = nc.dram_tensor("xtm_g", [S, H], F32, kind="ExternalInput").ap()
    d_xf_o = nc.dram_tensor("xf_o", [H, 1024], F32, kind="ExternalInput").ap()
    d_xtm_o = nc.dram_tensor("xtm_o", [1024, H], F32, kind="ExternalInput").ap()
    d_wq = nc.dram_tensor("wq_t", [18, 128, 2048], F32, kind="ExternalInput").ap()
    d_bqkv = nc.dram_tensor("bqkv_t", [128, 18], F32, kind="ExternalInput").ap()
    d_wp = nc.dram_tensor("wp_t", [16, 128, 2048], F32, kind="ExternalInput").ap()
    d_bproj = nc.dram_tensor("bproj_b", [128, 2048], F32, kind="ExternalInput").ap()
    d_gu = nc.dram_tensor("gu_t", [44, 128, 4096], F32, kind="ExternalInput").ap()
    d_gs = nc.dram_tensor("gs_t", [128, 44], F32, kind="ExternalInput").ap()
    d_us = nc.dram_tensor("us_t", [128, 44], F32, kind="ExternalInput").ap()
    d_dn = nc.dram_tensor("dn_t", [2, 44, 128, 1024], F32, kind="ExternalInput").ap()
    d_dsb = nc.dram_tensor("dsb_b", [128, 2048], F32, kind="ExternalInput").ap()
    d_msk = nc.dram_tensor("masks", [128, 8192], F32, kind="ExternalInput").ap()
    d_id = nc.dram_tensor("ident", [128, 128], F32, kind="ExternalInput").ap()
    d_out = nc.dram_tensor("out", [1024, H], F32, kind="ExternalOutput").ap()
    dbg_t = {}
    if debug:
        for nm, width in [("dbg_xn", 16384), ("dbg_q", 16384), ("dbg_k", 2048),
                          ("dbg_v", 2048), ("dbg_at", 16384), ("dbg_s", 16384),
                          ("dbg_hd", 22528)]:
            dbg_t[nm] = nc.dram_tensor(nm, [128, width], BF16,
                                       kind="ExternalOutput").ap()

    es = ExitStack()
    sb = lambda nm, shp, dt: es.enter_context(nc.sbuf_tensor(nm, shp, dt))
    pt = lambda nm, shp, dt: es.enter_context(nc.psum_tensor(nm, shp, dt))

    arena = sb("arena", [128, ARENA_E], BF16)
    wring = sb("wring", [128, 8192], BF16)
    wd = sb("wd", [128, 3072], BF16)
    xtm = sb("xtm", [128, 2 * 2048], F32)
    xf = sb("xf", [128, 4 * 512], F32)
    fscr = sb("fscr", [128, 8 * 512], F32)
    outt = sb("outt", [128, 2 * 512], F32)
    st24 = outt[:, 0:96]
    stat = outt[:, 96:224]
    rrow = sb("rrow", [1, 2048], F32)
    bqkv = sb("bqkv", [128, 18], F32)
    gsc = sb("gsc", [128, 44], F32)
    usc = sb("usc", [128, 44], F32)
    id32 = sb("id32", [128, 128], F32)
    ones_kk = sb("ones_kk", [128, 1], BF16)
    ones_b32 = sb("ones_b32", [1, 128], F32)

    ps = [pt(f"ps{i}", [128, 512], F32) for i in range(8)]

    sems = {nm: es.enter_context(nc.semaphore(f"s_{nm}"))
            for nm in ("tM", "aM", "vM", "pM", "cC", "cG",
                       "cXT0", "cXT1", "cXF0", "cXF1", "cXF2", "cXF3",
                       "cW0", "cW1", "cD0", "cD1", "cD2", "oS0", "oS1",
                       "cK0", "cK1")}
    cnt = {nm: 0 for nm in sems}
    EV = {e: [] for e in ENGS}

    def ev(eng, fn, waits=(), inc=None, k=1):
        if inc is not None:
            cnt[inc] += k
        EV[eng].append((tuple((s, w) for s, w in waits if w and w > 0),
                        fn, inc, k))
        return cnt[inc] if inc else None

    def dma(eng, out, in_, waits=(), inc=None):
        return ev(eng, lambda e, o=out, i=in_: e.dma_start(out=o, in_=i),
                  waits=waits, inc=inc, k=16)

    def fs(i):
        return fscr[:, i * 512:(i + 1) * 512]

    ev("gpsimd", lambda e: e.memset(ones_kk[:, :], 1.0), inc="pM")
    ev("gpsimd", lambda e: e.memset(ones_b32[:, :], 1.0), inc="pM")
    w_ones = cnt["pM"]
    for dst, src in [(bqkv, d_bqkv), (gsc, d_gs), (usc, d_us), (id32, d_id)]:
        dma("sync", dst[:, :], src[:, :], inc="cC")
    w_cst = ("cC", cnt["cC"])

    xtm_i = [0]
    xf_i = [0]
    xtm_free = {}
    xf_free = {}
    tg_i = [0]
    LN_DVE_END = {}
    TR_LAST = {}
    ROW_FREE = {}
    RB_FREE = {}
    LN_RB = {}
    LN_BC = {}
    LN_NORM_END = {}

    def emit_ln_stats(src_tm, tm_row0):
        tg = tg_i[0]
        tg_i[0] += 1
        par = tg % 2
        Q0 = 0
        P0 = 32 + par * 48
        st = stat
        first_stats = True
        for ccj in range(4):
            ci = xtm_i[0]
            xtm_i[0] += 1
            sp = ci % 2
            tmslot = xtm[:, sp * 2048:(sp + 1) * 2048]
            wts = [w_cst]
            if (ci - 2) in xtm_free:
                wts.append(("vM", xtm_free[ci - 2]))
            xtsem = f"cXT{sp}"
            dma("sync", tmslot,
                src_tm[tm_row0 + ccj * 128: tm_row0 + (ccj + 1) * 128, :],
                waits=wts, inc=xtsem)
            ld = (xtsem, cnt[xtsem])
            stw = [ld]
            if first_stats:
                first_stats = False
                if (tg - 1) in LN_DVE_END:
                    stw.append(("vM", LN_DVE_END[tg - 1]))
            for j in range(4):
                ev("vector",
                   lambda e, o=st24[:, Q0 + ccj * 24 + 6 * j:
                                    Q0 + ccj * 24 + 6 * (j + 1)],
                   i=tmslot[:, j * 512:(j + 1) * 512]: e.bn_stats(o, i),
                   waits=stw if j == 0 else (),
                   inc="vM" if j == 3 else None)
            xtm_free[ci] = cnt["vM"]
        def vstep(fn, extra=()):
            pv = cnt["vM"]
            ev("vector", fn, waits=[("vM", pv)] + list(extra), inc="vM")
        means32 = st24[:, Q0 + 1:Q0 + 96:3]
        m2s32 = st24[:, Q0 + 2:Q0 + 96:3]
        means3 = means32.rearrange("p (c m) -> p c m", m=8)
        m2s3 = m2s32.rearrange("p (c m) -> p c m", m=8)
        msq = stat[:, 0:32]
        msq3 = msq.rearrange("p (c m) -> p c m", m=8)
        c_summ = st[:, P0 + 0:P0 + 4]
        c_sumM2 = st[:, P0 + 4:P0 + 8]
        c_summsq = st[:, P0 + 8:P0 + 12]
        c_m = st[:, P0 + 12:P0 + 16]
        c_ex2a = st[:, P0 + 16:P0 + 20]
        c_t5 = st[:, P0 + 20:P0 + 24]
        c_ex2 = st[:, P0 + 24:P0 + 28]
        c_msq2 = st[:, P0 + 28:P0 + 32]
        c_var = st[:, P0 + 32:P0 + 36]
        c_lnv = st[:, P0 + 36:P0 + 40]
        c_rstd = st[:, P0 + 40:P0 + 44]
        c_mr = st[:, P0 + 44:P0 + 48]
        vstep(lambda e, o=c_summ, i=means3: e.reduce_sum(o, i, axis=AX.X),
              extra=[("tM", TR_LAST[tg - 2])] if (tg - 2) in TR_LAST else ())
        vstep(lambda e, o=c_sumM2, i=m2s3: e.reduce_sum(o, i, axis=AX.X))
        vstep(lambda e, o=msq, a=means32, b=means32:
              e.tensor_tensor(o, a, b, ALU.mult))
        vstep(lambda e, o=c_summsq, i=msq3: e.reduce_sum(o, i, axis=AX.X))
        vstep(lambda e, o=c_m, i=c_summ:
              e.tensor_scalar(o, i, 0.125, None, ALU.mult))
        vstep(lambda e, o=c_ex2a, i=c_sumM2:
              e.tensor_scalar(o, i, 1.0 / 2048.0, None, ALU.mult))
        vstep(lambda e, o=c_t5, i=c_summsq:
              e.tensor_scalar(o, i, 0.125, None, ALU.mult))
        vstep(lambda e, o=c_ex2, a=c_ex2a, b=c_t5:
              e.tensor_tensor(o, a, b, ALU.add))
        vstep(lambda e, o=c_msq2, a=c_m, b=c_m:
              e.tensor_tensor(o, a, b, ALU.mult))
        vstep(lambda e, o=c_var, a=c_ex2, b=c_msq2:
              e.tensor_tensor(o, a, b, ALU.subtract))
        vstep(lambda e, o=c_lnv, i=c_var:
              e.tensor_scalar(o, i, EPS, None, ALU.add))
        var_done = cnt["vM"]
        lnw = [("vM", var_done)]
        if (tg - 2) in TR_LAST:
            lnw.append(("tM", TR_LAST[tg - 2]))
        ev("scalar", lambda e, o=c_lnv, i=c_lnv:
           e.activation(o, i, AF.Ln), waits=lnw, inc="aM")
        _lv = cnt["aM"]
        ev("scalar", lambda e, o=c_rstd, i=c_lnv:
           e.activation(o, i, AF.Exp, scale=-0.5),
           waits=[("aM", _lv)], inc="aM")
        rstd_done = cnt["aM"]
        vstep(lambda e, o=c_mr, a=c_m, b=c_rstd:
              e.tensor_tensor(o, a, b, ALU.mult),
              extra=[("aM", rstd_done)])
        mr_done = cnt["vM"]
        LN_DVE_END[tg] = mr_done
        for ccj in range(4):
            trw = []
            if ccj == 0:
                trw = [("aM", rstd_done), ("vM", mr_done), w_cst]
                if (tg - 2) in ROW_FREE:
                    trw.append(("vM", ROW_FREE[tg - 2]))
                if (tg - 2) in LN_BC:
                    trw.append(("tM", LN_BC[tg - 2]))
            ev("tensor", lambda e, o=ps[2][0:1, ccj * 128:(ccj + 1) * 128],
               i=c_rstd[:, ccj:ccj + 1]: e.transpose(o, i, id32[:, :]),
               waits=trw)
            ev("tensor", lambda e, o=ps[3][0:1, ccj * 128:(ccj + 1) * 128],
               i=c_mr[:, ccj:ccj + 1]: e.transpose(o, i, id32[:, :]),
               inc="tM" if ccj == 3 else None)
        TR_LAST[tg] = cnt["tM"]
        tr_last = cnt["tM"]
        rsl = rrow[0:1, par * 512:(par + 1) * 512]
        mrl = rrow[0:1, 1024 + par * 512:1024 + (par + 1) * 512]
        ev("vector", lambda e, o=rsl, i=ps[2][0:1, 0:512]: e.tensor_copy(o, i),
           waits=[("tM", tr_last)], inc="vM")
        _rv = cnt["vM"]
        ev("vector", lambda e, o=mrl, i=ps[3][0:1, 0:512]: e.tensor_copy(o, i),
           waits=[("vM", _rv)], inc="vM")
        ROW_FREE[tg] = cnt["vM"]
        row_done = cnt["vM"]
        bw = [("vM", row_done), ("pM", w_ones)]
        if (tg - 1) in RB_FREE:
            bw.append(("vM", RB_FREE[tg - 1]))
        ev("tensor", lambda e, o=ps[4][:, :], l=ones_b32[:, :], r=rsl:
           e.matmul(o, l, r, start=True, stop=True), waits=bw)
        ev("tensor", lambda e, o=ps[5][:, :], l=ones_b32[:, :], r=mrl:
           e.matmul(o, l, r, start=True, stop=True), inc="tM")
        LN_BC[tg] = cnt["tM"]
        bc_v = cnt["tM"]
        rb_s, mrb_s = fs(par), fs(2 + par)
        rbw = [("tM", bc_v)]
        if (tg - 2) in LN_NORM_END:
            rbw.append(("vM", LN_NORM_END[tg - 2]))
        ev("vector", lambda e, o=rb_s, i=ps[4][:, :]: e.tensor_copy(o, i),
           waits=rbw, inc="vM")
        _rc = cnt["vM"]
        ev("vector", lambda e, o=mrb_s, i=ps[5][:, :]: e.tensor_copy(o, i),
           waits=[("vM", _rc)], inc="vM")
        RB_FREE[tg] = cnt["vM"]
        rb_done = cnt["vM"]
        LN_RB[tg] = (rb_s, mrb_s, cnt["vM"])
        return tg

    def emit_ln_norm(tg, src_f, f_col0, dst_fn, norm_extra=(),
                     hc_range=None, xn_done=None):
        rb_s, mrb_s, rb_done = LN_RB[tg]
        if xn_done is None:
            xn_done = []
        for hc in (hc_range if hc_range is not None else range(HC)):
            fi = xf_i[0]
            xf_i[0] += 1
            fsl = xf[:, (fi % 4) * 512:(fi % 4 + 1) * 512]
            fw = [w_cst]
            if (fi - 4) in xf_free:
                fw.append(("vM", xf_free[fi - 4]))
            xfsem = f"cXF{fi % 4}"
            dma("sync", fsl, src_f[hc * 128:(hc + 1) * 128, f_col0:f_col0 + 512],
                waits=fw, inc=xfsem)
            fl = (xfsem, cnt[xfsem])
            t1 = fs(4 + hc % 2)
            t1w = [fl, ("vM", rb_done)]
            if hc >= 2:
                t1w.append(("vM", xn_done[hc - 2]))
            ev("vector", lambda e, o=t1, a=fsl, b=rb_s:
               e.tensor_tensor(o, a, b, ALU.mult),
               waits=t1w + (list(norm_extra) if hc == 0 else []), inc="vM")
            t1v_ = cnt["vM"]
            ev("vector", lambda e, o=dst_fn(hc), a=t1, b=mrb_s:
               e.tensor_tensor(o, a, b, ALU.subtract),
               waits=[("vM", t1v_)], inc="vM")
            xf_free[fi] = cnt["vM"]
            xn_done.append(cnt["vM"])
        if len(xn_done) == HC:
            LN_NORM_END[tg] = cnt["vM"]
        return xn_done

    xn_ready = {}

    def emit_norm_own(g):
        vals = emit_ln_norm(
            g, d_xf_o, g * 512,
            lambda hc, g=g: arena[:, A_XN + hc * 1024 + g * 512:
                                  A_XN + hc * 1024 + g * 512 + 512])
        for hc in range(HC):
            xn_ready[(hc, g)] = vals[hc]

    emit_ln_stats(d_xtm_o, 0)
    emit_ln_stats(d_xtm_o, 512)
    emit_norm_own(0)

    dma("gpsimd", wring[:, 0:2048], d_wq[16], inc="cK0")
    kv_k_ld = ("cK0", cnt["cK0"])
    dma("gpsimd", wring[:, 2048:4096], d_wq[17], inc="cK1")
    kv_v_ld = ("cK1", cnt["cK1"])

    wq_load = {}
    wq_free = {}
    q_done = {}
    k_done = {}
    v_done = {}
    kv_prev = {"K": None, "V": None}
    vstage_free = {}

    def emit_q_chunk(n):
        slot = wring[:, 4096 + (n % 2) * 2048: 4096 + (n % 2 + 1) * 2048]
        lw = []
        if (n - 2) in wq_free:
            lw.append(("tM", wq_free[n - 2]))
        wsem = f"cW{n % 2}"
        dma("gpsimd", slot, d_wq[n], waits=lw, inc=wsem)
        wq_load[n] = (wsem, cnt[wsem])
        for g in range(2):
            bank = ps[(n * 2 + g) % 2]
            xn_all_g = max(xn_ready[(hc, g)] for hc in range(HC))
            for hc in range(HC):
                if hc == 0:
                    wts = [wq_load[n], ("vM", xn_all_g)]
                    if (n - 1, g) in q_done:
                        wts.append(("aM", q_done[(n - 1, g)]))
                else:
                    wts = ()
                ev("tensor", lambda e, o=bank[:, :],
                   l=slot[:, hc * 128:(hc + 1) * 128],
                   r=arena[:, A_XN + hc * 1024 + g * 512:
                           A_XN + hc * 1024 + g * 512 + 512],
                   st=(hc == 0), sp=(hc == HC - 1):
                   e.matmul(o, l, r, start=st, stop=sp),
                   waits=wts, inc="tM" if hc == HC - 1 else None)
            mm_done = cnt["tM"]
            if g == 1:
                wq_free[n] = mm_done
            ev("scalar", lambda e,
               o=arena[:, A_Q + n * 1024 + g * 512:
                       A_Q + n * 1024 + g * 512 + 512],
               i=bank[:, :], b=bqkv[:, n:n + 1]:
               e.activation(o, i, AF.Identity, bias=b),
               waits=[("tM", mm_done), w_cst], inc="aM")
            q_done[(n, g)] = cnt["aM"]

    def emit_kv_tg(tg, xn_vals, xn_fn):
        for nn, nm in ((16, "K"), (17, "V")):
            bank = ps[6] if nm == "K" else ps[7]
            xn_all = max(xn_vals)
            for hc in range(HC):
                wts = [("vM", xn_vals[hc])] if nm == "K" and hc > 0 else ()
                if hc == 0:
                    wts = [("vM", xn_vals[0] if nm == "K" else xn_all),
                           kv_k_ld if nm == "K" else kv_v_ld]
                    if kv_prev[nm] is not None:
                        wts.append(kv_prev[nm])
                    if tg == 2:
                        if nm == "K":
                            wts.append(("aM", ex_done[NB_A - 1]))
                        else:
                            wts.append(("vM", rbA_done[13]))
                    elif tg == 3:
                        if nm == "K":
                            wts.append(("aM", ex_done[NB_A + 7]))
                        else:
                            wts.append(("vM", rbA_done[14]))
                ev("tensor", lambda e, o=bank[:, :],
                   l=wring[:, (nn - 16) * 2048 + hc * 128:
                           (nn - 16) * 2048 + (hc + 1) * 128],
                   r=xn_fn(hc), st=(hc == 0), sp=(hc == HC - 1):
                   e.matmul(o, l, r, start=st, stop=sp),
                   waits=wts, inc="tM" if hc == HC - 1 else None)
            mm_done = cnt["tM"]
            if nm == "K":
                ev("scalar", lambda e,
                   o=arena[:, A_K + tg * 512:A_K + (tg + 1) * 512],
                   i=bank[:, :], b=bqkv[:, 16:17]:
                   e.activation(o, i, AF.Identity, bias=b),
                   waits=[("tM", mm_done), w_cst], inc="aM")
                k_done[tg] = cnt["aM"]
                kv_prev["K"] = ("aM", cnt["aM"])
            else:
                vst = fs(6)
                vw = [("tM", mm_done), w_cst]
                if (tg - 1) in vstage_free:
                    vw.append(("tM", vstage_free[tg - 1]))
                ev("scalar", lambda e, o=vst, i=bank[:, :], b=bqkv[:, 17:18]:
                   e.activation(o, i, AF.Identity, bias=b),
                   waits=vw, inc="aM")
                vev = cnt["aM"]
                for j in range(4):
                    ev("tensor", lambda e, o=ps[7][:, j * 128:(j + 1) * 128],
                       i=vst[:, j * 128:(j + 1) * 128]:
                       e.transpose(o, i, id32[:, :]),
                       waits=[("aM", vev)] if j == 0 else (),
                       inc="tM" if j == 3 else None)
                vstage_free[tg] = cnt["tM"]
                tr_done = cnt["tM"]
                ev("vector", lambda e,
                   o=arena[:, A_V + tg * 512:A_V + (tg + 1) * 512],
                   i=ps[7][:, 0:512]: e.tensor_copy(o, i),
                   waits=[("tM", tr_done)], inc="vM")
                v_done[tg] = cnt["vM"]
                kv_prev["V"] = ("vM", cnt["vM"])

    def emit_key_norm_kv(ktg):
        tgid = 2 + ktg
        base = 0 if ktg == 0 else 8192
        extra = [("tM", vstage_free[ktg - 1])] if ktg >= 2 else []
        xn_fn = lambda hc, base=base: arena[:, A_AT + base + hc * 512:
                                            A_AT + base + (hc + 1) * 512]
        vals = emit_ln_norm(tgid, d_xf_g, ktg * 512, xn_fn, norm_extra=extra)
        emit_kv_tg(ktg, vals, xn_fn)

    emit_ln_stats(d_xtm_g, 0)
    emit_norm_own(1)
    for n in range(0, 8):
        emit_q_chunk(n)
    emit_ln_stats(d_xtm_g, 512)
    emit_key_norm_kv(0)
    for n in range(8, 12):
        emit_q_chunk(n)
    dma("gpsimd", arena[:, A_MSK:A_MSK + 8192], d_msk[:, :], inc="cG")
    dma("gpsimd", arena[:, A_BC:A_BC + 2048], d_bproj[:, :], inc="cG")
    dma("gpsimd", arena[:, A_BC + 2048:A_BC + 4096], d_dsb[:, :], inc="cG")
    w_msk = w_bproj = w_dsb = ("cG", cnt["cG"])

    emit_ln_stats(d_xtm_g, 1024)
    emit_key_norm_kv(1)
    for n in range(12, 16):
        emit_q_chunk(n)
    emit_ln_stats(d_xtm_g, 1536)

    groups = [(h, 0) for h in range(NH)] + [(h, 1) for h in range(NH)]
    blocks = []
    for hi, (h, g) in enumerate(groups):
        for kc in range(NKC[g]):
            blocks.append((hi, h, g, kc))
    NB = len(blocks)

    sc_done = [0] * NB
    ex_done = [0] * NB
    pd_done = [0] * NB
    pd_sem = [None] * NB
    pv_done = [0] * NB
    rc_done = {}
    bcA_done = {}
    rbA_done = {}
    at_done = {}
    grp_sums_stop = {}
    grp_pv_stop = {}

    q_last = q_done[(15, 1)]
    rowfree_last = ROW_FREE[tg_i[0] - 1]
    rbfree_last = RB_FREE[tg_i[0] - 1]

    def blk_masked(g, kc):
        return g == 0 or kc >= 8

    def mask_slot(g, kc):
        return g * 8 + (kc if g == 0 else kc - 8)

    SC_BANKS = (ps[0], ps[1], ps[6])

    def emit_scores(t):
        hi, h, g, kc = blocks[t]
        bank = SC_BANKS[t % 3]
        wts = []
        if kc == 0:
            wts.append(("aM", q_done[(h, g)]))
        if kc % 4 == 0:
            wts.append(("aM", k_done[kc // 4]))
        if t >= 3:
            wts.append(("aM", ex_done[t - 3]))
        elif t < 2:
            wts.append(("aM", q_last))
        if NB_A <= t < NB_A + 3:
            wts.append(("aM", k_done[2]))
            wts.append(("vM", v_done[2]))
        if NB_A + 8 <= t < NB_A + 11:
            wts.append(("aM", k_done[3]))
            wts.append(("vM", v_done[3]))
        ev("tensor", lambda e, o=bank[:, :],
           l=arena[:, A_K + kc * 128:A_K + (kc + 1) * 128],
           r=arena[:, A_Q + h * 1024 + g * 512:A_Q + h * 1024 + (g + 1) * 512]:
           e.matmul(o, l, r, start=True, stop=True),
           waits=wts, inc="tM")
        sc_done[t] = cnt["tM"]

    def emit_exp(t):
        hi, h, g, kc = blocks[t]
        bank = SC_BANKS[t % 3]
        slot = arena[:, A_PR + (t % R_P) * 512:A_PR + (t % R_P + 1) * 512]
        wts = [("tM", sc_done[t])]
        if t >= R_P:
            wts.append(("tM", pv_done[t - R_P]))
        ev("scalar", lambda e, o=slot, i=bank[:, :]:
           e.activation(o, i, AF.Exp, scale=SCALE),
           waits=wts, inc="aM")
        ex_done[t] = cnt["aM"]
        if blk_masked(g, kc):
            ms = mask_slot(g, kc)
            ev("vector", lambda e, o=slot, a=slot,
               b=arena[:, A_MSK + ms * 512:A_MSK + (ms + 1) * 512]:
               e.tensor_tensor(o, a, b, ALU.mult),
               waits=[("aM", ex_done[t]), w_msk], inc="vM")
            pd_done[t], pd_sem[t] = cnt["vM"], "vM"
        else:
            pd_done[t], pd_sem[t] = ex_done[t], "aM"

    def emit_sums_pv(t):
        hi, h, g, kc = blocks[t]
        slot = arena[:, A_PR + (t % R_P) * 512:A_PR + (t % R_P + 1) * 512]
        sums_bank = ps[4 + hi % 2]
        pv_bank = ps[2 + hi % 2]
        first, last = kc == 0, kc == NKC[g] - 1
        wts = [(pd_sem[t], pd_done[t])]
        if first:
            wts.append(("pM", w_ones))
            if (hi - 2) in rc_done:
                wts.append(rc_done[hi - 2])
            elif hi < 2:
                wts.append(("vM", rbfree_last))
        ev("tensor", lambda e, o=sums_bank[0:1, :], l=ones_kk[:, :], r=slot,
           st=first, sp=last: e.matmul(o, l, r, start=st, stop=sp),
           waits=wts, inc="tM" if last else None)
        if last:
            grp_sums_stop[hi] = cnt["tM"]
        pw = []
        if kc % 4 == 0:
            pw.append(("vM", v_done[kc // 4]))
        if first:
            if (hi - 2) in at_done:
                pw.append(("vM", at_done[hi - 2]))
            elif hi < 2:
                pw.append(("vM", rowfree_last))
        ev("tensor", lambda e, o=pv_bank[:, :],
           l=arena[:, A_V + kc * 128:A_V + (kc + 1) * 128],
           r=slot, st=first, sp=last:
           e.matmul(o, l, r, start=st, stop=sp),
           waits=pw, inc="tM")
        pv_done[t] = cnt["tM"]
        if last:
            grp_pv_stop[hi] = cnt["tM"]

    def emit_tail_dve(hi):
        par = hi % 2
        rsl = outt[0:1, par * 512:(par + 1) * 512]
        lsl = outt[32:33, par * 512:(par + 1) * 512]
        ev("scalar", lambda e, o=lsl, i=ps[4 + par][0:1, :]:
           e.activation(o, i, AF.Ln),
           waits=[("tM", grp_sums_stop[hi])], inc="aM")
        _lv = cnt["aM"]
        ev("scalar", lambda e, o=rsl, i=lsl:
           e.activation(o, i, AF.Exp, scale=-1.0),
           waits=[("aM", _lv)], inc="aM")
        rc_done[hi] = ("aM", cnt["aM"])

    def emit_tail_bcast(hi):
        par = hi % 2
        rsl = outt[0:1, par * 512:(par + 1) * 512]
        wts = [rc_done[hi]]
        if (hi - 1) in rbA_done:
            wts.append(("vM", rbA_done[hi - 1]))
        if hi == 0:
            wts.append(("tM", vstage_free[1]))
        if hi == 14:
            wts.append(("tM", vstage_free[2]))
            wts.append(("vM", v_done[2]))
        if hi in (15, 16):
            wts.append(("tM", vstage_free[3]))
            wts.append(("vM", v_done[3]))
        ev("tensor", lambda e, o=ps[7][:, :], l=ones_b32[:, :], r=rsl:
           e.matmul(o, l, r, start=True, stop=True),
           waits=wts, inc="tM")
        bcA_done[hi] = cnt["tM"]

    def emit_tail_evac(hi):
        par = hi % 2
        h, g = groups[hi]
        rbs = xtm[:, par * 512:(par + 1) * 512]
        rcw = [("tM", bcA_done[hi])]
        if (hi - 2) in at_done:
            rcw.append(("vM", at_done[hi - 2]))
        ev("vector", lambda e, o=rbs, i=ps[7][:, :]: e.tensor_copy(o, i),
           waits=rcw, inc="vM")
        rbA_done[hi] = cnt["vM"]
        ev("vector", lambda e,
           o=arena[:, A_AT + g * 8192 + h * 512:
                   A_AT + g * 8192 + (h + 1) * 512],
           a=ps[2 + par][:, :], b=rbs:
           e.tensor_tensor(o, a, b, ALU.mult),
           waits=[("tM", grp_pv_stop[hi]), ("vM", rbA_done[hi])], inc="vM")
        at_done[hi] = cnt["vM"]

    NB_A = sum(NKC[0] for _ in range(NH))
    key_vals = {}
    key_xfn = {}

    def emit_key_norm_only(ktg):
        tgid = 2 + ktg
        base = 0 if ktg == 0 else 8192
        extra = [("tM", vstage_free[ktg - 1])] if ktg >= 2 else []
        xn_fn = lambda hc, base=base: arena[:, A_AT + base + hc * 512:
                                            A_AT + base + (hc + 1) * 512]
        key_vals[ktg] = emit_ln_norm(tgid, d_xf_g, ktg * 512, xn_fn,
                                     norm_extra=extra)
        key_xfn[ktg] = xn_fn

    def emit_key_norm_piece(ktg, piece):
        tgid = 2 + ktg
        base = 0 if ktg == 0 else 8192
        extra = ([("tM", vstage_free[ktg - 1])]
                 if (ktg >= 2 and piece == 0) else [])
        xn_fn = lambda hc, base=base: arena[:, A_AT + base + hc * 512:
                                            A_AT + base + (hc + 1) * 512]
        key_xfn[ktg] = xn_fn
        acc = key_vals.setdefault(ktg, [])
        emit_ln_norm(tgid, d_xf_g, ktg * 512, xn_fn, norm_extra=extra,
                     hc_range=range(piece * 4, (piece + 1) * 4), xn_done=acc)

    pend = None
    for t in range(NB):
        hi, h, g, kc = blocks[t]
        if NB_A - 40 <= t < NB_A - 40 + 32 and (t - (NB_A - 40)) % 8 == 0:
            emit_key_norm_piece(2, (t - (NB_A - 40)) // 8)
        if t == NB_A:
            emit_kv_tg(2, key_vals[2], key_xfn[2])
        if NB_A <= t < NB_A + 8 and (t - NB_A) % 2 == 0:
            emit_key_norm_piece(3, (t - NB_A) // 2)
        if t == NB_A + 8:
            emit_kv_tg(3, key_vals[3], key_xfn[3])
        emit_scores(t)
        if kc == 1 and pend is not None:
            emit_tail_bcast(pend)
            emit_tail_evac(pend)
            pend = None
        emit_exp(t)
        if t >= 2:
            emit_sums_pv(t - 2)
            phi, _, pg, pkc = blocks[t - 2]
            if pkc == NKC[pg] - 1:
                emit_tail_dve(phi)
                pend = phi
    for tt in (NB - 2, NB - 1):
        emit_sums_pv(tt)
        phi, _, pg, pkc = blocks[tt]
        if pkc == NKC[pg] - 1:
            emit_tail_dve(phi)
            if pend is not None and pend != phi:
                emit_tail_bcast(pend)
                emit_tail_evac(pend)
            pend = phi
    if pend is not None and pend != len(groups) - 1:
        emit_tail_bcast(pend)
        emit_tail_evac(pend)
    emit_tail_bcast(len(groups) - 1)
    emit_tail_evac(len(groups) - 1)

    wp_load = {}
    wp_free = {}
    s_done = {}
    pj_free = {}
    attn_all = max(at_done[len(groups) - 1], rbA_done[len(groups) - 1])
    exp_all = max(ex_done[NB - 1], rc_done[len(groups) - 1][1])

    for sw in range(4):
        fgh, qq = sw // 2, sw % 2
        for a in range(16):
            ap_ = a // 2
            slot = wring[:, (ap_ % 4) * 2048:(ap_ % 4 + 1) * 2048]
            if a % 2 == 0:
                lw = []
                prev = (sw, a - 7) if a >= 8 else \
                       ((sw - 1, 9 + a) if sw > 0 else None)
                if prev in wp_free:
                    lw.append(("tM", wp_free[prev]))
                if sw == 0 and a < 8:
                    lw.append(("tM", vstage_free[3]))
                wsem = ("cW0", "cW1", "cK0", "cK1")[ap_ % 4]
                dma("gpsimd",
                    slot.rearrange("p (i f) -> p i f", f=1024),
                    d_wp[a:a + 2][:, :, fgh * 1024:(fgh + 1) * 1024]
                    .transpose([1, 0, 2]),
                    waits=lw, inc=wsem)
                wp_load[(sw, a)] = (wsem, cnt[wsem])
                wp_load[(sw, a + 1)] = (wsem, cnt[wsem])
            first_mm = True
            for qc_l in range(4):
                qcg = qq * 4 + qc_l
                g = qcg // 4
                for fgi in range(2):
                    bank = ps[qc_l * 2 + fgi]
                    wts = ()
                    if first_mm:
                        first_mm = False
                        wts = [wp_load[(sw, a)],
                               ("vM", max(at_done[groups.index((a, 0))], at_done[groups.index((a, 1))]))]
                        if a == 0:
                            if sw >= 1:
                                wts.append(("vM", max(
                                    pj_free[(sw - 1, b)] for b in range(8))))
                            else:
                                wts.append(("aM", exp_all))
                                wts.append(("vM", attn_all))
                    ev("tensor", lambda e, o=bank[:, :],
                       l=arena[:, A_AT + g * 8192 + a * 512 + (qcg % 4) * 128:
                               A_AT + g * 8192 + a * 512 + (qcg % 4 + 1) * 128],
                       r=slot[:, (a % 2) * 1024 + fgi * 512:
                              (a % 2) * 1024 + (fgi + 1) * 512],
                       st=(a == 0), sp=(a == 15):
                       e.matmul(o, l, r, start=st, stop=sp),
                       waits=wts,
                       inc="tM" if (qc_l == 3 and fgi == 1) else None)
            wp_free[(sw, a)] = cnt["tM"]
        sweep_done = cnt["tM"]
        for qc_l in range(4):
            qcg = qq * 4 + qc_l
            for fgi in range(2):
                fg = fgh * 2 + fgi
                bank = ps[qc_l * 2 + fgi]
                ev("vector", lambda e,
                   o=arena[:, A_Q + qcg * 2048 + fg * 512:
                           A_Q + qcg * 2048 + (fg + 1) * 512],
                   a=bank[:, :],
                   b=arena[:, A_BC + fg * 512:A_BC + (fg + 1) * 512]:
                   e.tensor_tensor(o, a, b, ALU.add),
                   waits=[("tM", sweep_done), w_bproj], inc="vM")
                s_done[(qcg, fg)] = cnt["vM"]
                pj_free[(sw, qc_l * 2 + fgi)] = cnt["vM"]
    pj_all = max(pj_free.values())

    gu_load = {}
    gu_free = {}
    hdn_done = {}
    attn_dve_end = attn_all
    dn_load = {}
    dn_free = {}
    dn_slot_prev = {}
    epi_t1 = {}
    epi_gv = {}
    epi_t2v = {}
    epi_outv = {}
    gv_last = {}
    out_last = {}
    xtile_free = {}
    store_v = {}
    out_i = [0]
    last_down = [0]

    for g in range(2):
        sg_free = {}
        us_free = {}
        for i in range(IC):
            slot = wring[:, (i % 2) * 4096:(i % 2 + 1) * 4096]
            key = (g, i)
            ld2 = []
            for half in range(2):
                prev = (g, i - 2) if i >= 2 else \
                       ((g - 1, IC - 2 + i) if g > 0 else None)
                lw = []
                if half == 0:
                    if prev in gu_free:
                        lw.append(("tM", gu_free[prev]))
                    if g == 0 and i < 2:
                        lw.append(("tM", wp_free[(3, 15)]))
                wsem = f"cW{(2 * i + half) % 4}" if False else f"cW{i % 2}"
                hsem = ("cW0", "cW1", "cK0", "cK1")[(2 * i + half) % 4]
                dma("gpsimd", slot[:, half * 2048:(half + 1) * 2048],
                    d_gu[i][:, half * 2048:(half + 1) * 2048],
                    waits=lw, inc=hsem)
                ld2.append((hsem, cnt[hsem]))
            gu_load[key] = ld2
            gu_load2 = ld2
            gb = ps[0 + (i % 2) * 2]
            ub = ps[1 + (i % 2) * 2]
            for hc in range(HC):
                wts = ()
                if hc == 0:
                    wts = [gu_load[key][0]]
                    if prev in sg_free:
                        wts.append(("aM", sg_free[prev]))
                    if g == 0 and i < 2:
                        wts.append(("vM", pj_all))
                    if g == 1 and i == 0:
                        wts.append(("tM", last_down[0]))
                        wts.append(("vM", epi_t1[(0, 3, 3)]))
                ev("tensor", lambda e, o=gb[:, :],
                   l=slot[:, hc * 128:(hc + 1) * 128],
                   r=arena[:, A_XN + hc * 1024 + g * 512:
                           A_XN + hc * 1024 + g * 512 + 512],
                   st=(hc == 0), sp=(hc == HC - 1):
                   e.matmul(o, l, r, start=st, stop=sp),
                   waits=wts, inc="tM" if hc == HC - 1 else None)
            g_stop = cnt["tM"]
            for hc in range(HC):
                wts = ()
                if hc == 0:
                    wts = [gu_load[key][1]]
                    if prev in us_free:
                        wts.append(("vM", us_free[prev]))
                    if g == 0 and i < 2:
                        wts.append(("vM", pj_all))
                ev("tensor", lambda e, o=ub[:, :],
                   l=slot[:, 2048 + hc * 128:2048 + (hc + 1) * 128],
                   r=arena[:, A_XN + hc * 1024 + g * 512:
                           A_XN + hc * 1024 + g * 512 + 512],
                   st=(hc == 0), sp=(hc == HC - 1):
                   e.matmul(o, l, r, start=st, stop=sp),
                   waits=wts, inc="tM" if hc == HC - 1 else None)
            u_stop = cnt["tM"]
            gu_free[key] = cnt["tM"]
            sgs = fs(4 + i % 2)
            uss = fs(6 + i % 2)
            sgw = [("tM", g_stop), w_cst]
            if prev in hdn_done:
                sgw.append(("vM", hdn_done[prev]))
            ev("scalar", lambda e, o=sgs, i_=gb[:, :], sc=gsc[:, i:i + 1]:
               e.activation(o, i_, AF.Silu, scale=sc),
               waits=sgw, inc="aM")
            sg_free[key] = cnt["aM"]
            sg_v = cnt["aM"]
            usw = [("tM", u_stop), w_cst]
            if prev in hdn_done:
                usw.append(("vM", hdn_done[prev]))
            ev("vector", lambda e, o=uss, i_=ub[:, :], sc=usc[:, i:i + 1]:
               e.tensor_scalar(o, i_, sc, None, ALU.mult),
               waits=usw, inc="vM")
            us_free[key] = cnt["vM"]
            hw = [("aM", sg_v), ("vM", us_free[key])]
            if g == 1 and i == 0:
                hw.append(("tM", last_down[0]))
            if g == 0 and i == 32:
                hw.append(("vM", attn_dve_end))
            ev("vector", lambda e,
               o=arena[:, A_HDN + i * 512:A_HDN + (i + 1) * 512],
               a=sgs, b=uss: e.tensor_tensor(o, a, b, ALU.mult),
               waits=hw, inc="vM")
            hdn_done[(g, i)] = cnt["vM"]

        for fgp in range(2):
            def emit_dn_load(i, fgp=fgp, g=g):
                key = (g, fgp, i)
                slot = i % 3
                pkey = dn_slot_prev.get(slot)
                dn_slot_prev[slot] = key
                lw = []
                if pkey is not None:
                    assert pkey in dn_free, f"dn pacing missing {pkey}"
                    lw.append(("tM", dn_free[pkey]))
                dsem = f"cD{slot}"
                dma("gpsimd",
                    wd[:, slot * 1024:(slot + 1) * 1024],
                    d_dn[fgp][i],
                    waits=lw, inc=dsem)
                dn_load[key] = (dsem, cnt[dsem])
            for i in range(3):
                emit_dn_load(i)
            for i in range(IC):
                for qc_l in (2, 3, 0, 1):
                    for fgi in range(2):
                        bank = ps[qc_l * 2 + fgi]
                        wts = []
                        if qc_l == 2 and fgi == 0:
                            wts.append(("vM", hdn_done[(g, i)]))
                            wts.append(dn_load[(g, fgp, i)])
                        if i == 0 and fgi == 0 and qc_l == 0:
                            wts.append(("aM", sg_free[(g, IC - 1)]))
                            wts.append(("vM", us_free[(g, IC - 1)]))
                        if i == 0:
                            fg = fgp * 2 + fgi
                            pk = (g, fg - 2, qc_l) if fgp > 0 else \
                                ((g - 1, fg + 2, qc_l) if g > 0 else None)
                            if pk in epi_t1:
                                wts.append(("vM", epi_t1[pk]))
                            elif g == 0 and fgp == 0 and qc_l == 0 and fgi == 0:
                                wts.append(("vM", pj_all))
                        ev("tensor", lambda e, o=bank[:, :],
                           l=arena[:, A_HDN + i * 512 + qc_l * 128:
                                   A_HDN + i * 512 + (qc_l + 1) * 128],
                           r=wd[:, (i % 3) * 1024 + fgi * 512:
                                (i % 3) * 1024 + (fgi + 1) * 512],
                           st=(i == 0), sp=(i == IC - 1):
                           e.matmul(o, l, r, start=st, stop=sp),
                           waits=wts,
                           inc="tM" if (qc_l == 1 and fgi == 1) else None)
                dn_free[(g, fgp, i)] = cnt["tM"]
                if i + 3 < IC:
                    emit_dn_load(i + 3)
            dn_stop = cnt["tM"]
            last_down[0] = dn_stop
            for fgi in range(2):
                fg = fgp * 2 + fgi
                xls = {}
                for qc_l in range(4):
                    fsl = xf[:, qc_l * 512:(qc_l + 1) * 512]
                    pk = (g, fg - 1, qc_l) if fg > 0 else \
                        ((g - 1, 3, qc_l) if g > 0 else None)
                    xw = []
                    if pk in xtile_free:
                        xw.append(("vM", xtile_free[pk]))
                    elif g == 0 and fg == 0:
                        xw.append(("vM", xf_free[xf_i[0] - 1]))
                    xfsem = f"cXF{qc_l}"
                    dma("sync", fsl,
                        d_xtm_o[g * 512 + qc_l * 128:g * 512 + (qc_l + 1) * 128,
                                fg * 512:(fg + 1) * 512],
                        waits=xw, inc=xfsem)
                    xls[qc_l] = (xfsem, cnt[xfsem])
                for qc_l in (2, 3, 0, 1):
                    bank = ps[qc_l * 2 + fgi]
                    t1 = fs(4 + qc_l % 2)
                    t2 = fs(6 + qc_l % 2)
                    oi = out_i[0]
                    out_i[0] += 1
                    osl = outt[:, (oi % 2) * 512:(oi % 2 + 1) * 512]
                    t1w = [("tM", dn_stop), w_dsb]
                    if qc_l % 2 in gv_last:
                        t1w.append(("aM", gv_last[qc_l % 2]))
                    ev("vector", lambda e, o=t1, a=bank[:, :],
                       b=arena[:, A_BC + 2048 + fg * 512:
                               A_BC + 2048 + (fg + 1) * 512]:
                       e.tensor_tensor(o, a, b, ALU.mult),
                       waits=t1w, inc="vM")
                    t1v = cnt["vM"]
                    epi_t1[(g, fg, qc_l)] = t1v
                    gsl = arena[:, A_PR + qc_l * 512:A_PR + (qc_l + 1) * 512]
                    gvw = [("vM", t1v)]
                    pk3 = (g, fg - 1, qc_l) if fg > 0 else \
                        ((g - 1, 3, qc_l) if g > 0 else None)
                    if pk3 in epi_t2v:
                        gvw.append(("vM", epi_t2v[pk3]))
                    ev("scalar", lambda e, o=gsl, i_=t1:
                       e.activation(o, i_, AF.Silu),
                       waits=gvw, inc="aM")
                    gv = cnt["aM"]
                    epi_gv[(g, fg, qc_l)] = gv
                    gv_last[qc_l % 2] = gv
                    qcg = g * 4 + qc_l
                    ev("vector", lambda e, o=t2, a=gsl,
                       b=arena[:, A_Q + qcg * 2048 + fg * 512:
                               A_Q + qcg * 2048 + (fg + 1) * 512]:
                       e.tensor_tensor(o, a, b, ALU.mult),
                       waits=[("aM", gv), ("vM", s_done[(qcg, fg)])] +
                             ([("vM", out_last[qc_l % 2])]
                              if qc_l % 2 in out_last else []),
                       inc="vM")
                    epi_t2v[(g, fg, qc_l)] = cnt["vM"]
                    ow = [xls[qc_l], ("vM", cnt["vM"])]
                    if oi >= 2:
                        ow.append(store_v[oi - 2])
                    ev("vector", lambda e, o=osl, a=t2,
                       b=xf[:, qc_l * 512:(qc_l + 1) * 512]:
                       e.tensor_tensor(o, a, b, ALU.add),
                       waits=ow, inc="vM")
                    ov = cnt["vM"]
                    epi_outv[(g, fg, qc_l)] = ov
                    out_last[qc_l % 2] = ov
                    xtile_free[(g, fg, qc_l)] = ov
                    osem = f"oS{oi % 2}"
                    dma("sync",
                        d_out[g * 512 + qc_l * 128:g * 512 + (qc_l + 1) * 128,
                              fg * 512:(fg + 1) * 512],
                        osl, waits=[("vM", ov)], inc=osem)
                    store_v[oi] = (osem, cnt[osem])

    if debug:
        allw = [("vM", cnt["vM"]), ("aM", cnt["aM"]), ("tM", cnt["tM"])]
        for nm, off, width in [("dbg_xn", A_XN, 16384), ("dbg_k", A_K, 2048),
                               ("dbg_v", A_V, 2048), ("dbg_at", A_AT, 16384),
                               ("dbg_s", A_Q, 16384), ("dbg_q", A_Q, 16384),
                               ("dbg_hd", A_HDN, 22528)]:
            dma("sync", dbg_t[nm][:, :], arena[:, off:off + width],
                waits=allw, inc="oS0")
            allw = []

    ev("sync", lambda e: e.wait_ge(sems["oS0"], cnt["oS0"]))
    ev("sync", lambda e: e.wait_ge(sems["oS1"], cnt["oS1"]))

    with es:
        with nc.Block() as blk:
            def mk_body(eng_nm):
                evs = EV[eng_nm]
                def body(e):
                    for waits, fn, incn, k in evs:
                        for snm, w in waits:
                            e.wait_ge(sems[snm], w)
                        instr = fn(e)
                        if incn is not None:
                            instr.then_inc(sems[incn], k)
                return body
            blk.gpsimd(mk_body("gpsimd"))
            blk.sync(mk_body("sync"))
            blk.tensor(mk_body("tensor"))
            blk.scalar(mk_body("scalar"))
            blk.vector(mk_body("vector"))
    return nc


def _prep_shared(W_qkv, b_qkv, W_proj, b_proj, gate_w, up_w, down_w,
                 gate_s, up_s, down_s):
    f32 = np.float32
    W_qkv = np.ascontiguousarray(np.asarray(W_qkv, f32))
    wq_t = np.ascontiguousarray(
        W_qkv.reshape(16, 128, 18, 128).transpose(2, 1, 0, 3)).reshape(
            18, 128, 2048)
    bqkv_t = np.ascontiguousarray(np.asarray(b_qkv, f32).reshape(18, 128).T)
    wp_t = np.ascontiguousarray(np.asarray(W_proj, f32)).reshape(16, 128, 2048)
    bproj_b = np.ascontiguousarray(
        np.broadcast_to(np.asarray(b_proj, f32), (128, 2048)))
    gt = np.ascontiguousarray(
        np.asarray(gate_w, f32).reshape(44, 128, 16, 128).transpose(0, 3, 2, 1)
    ).reshape(44, 128, 2048)
    ut = np.ascontiguousarray(
        np.asarray(up_w, f32).reshape(44, 128, 16, 128).transpose(0, 3, 2, 1)
    ).reshape(44, 128, 2048)
    gu_t = np.ascontiguousarray(np.concatenate([gt, ut], axis=2))
    gs_t = np.ascontiguousarray(np.asarray(gate_s, f32).reshape(44, 128).T)
    us_t = np.ascontiguousarray(np.asarray(up_s, f32).reshape(44, 128).T)
    dn = np.ascontiguousarray(np.asarray(down_w, f32).T).reshape(44, 128, 2048)
    dn_t = np.ascontiguousarray(
        dn.reshape(44, 128, 2, 1024).transpose(2, 0, 1, 3))
    dsb_b = np.ascontiguousarray(
        np.broadcast_to(np.asarray(down_s, f32), (128, 2048)))
    ident = np.eye(128, dtype=f32)
    return dict(wq_t=wq_t, bqkv_t=bqkv_t, wp_t=wp_t, bproj_b=bproj_b,
                gu_t=gu_t, gs_t=gs_t, us_t=us_t, dn_t=dn_t, dsb_b=dsb_b,
                ident=ident)


def _masks_for(sub):
    qstarts = (0, 1536) if sub == 0 else (512, 1024)
    m = np.zeros((128, 16 * 512), dtype=np.float32)
    kk = np.arange(128)[:, None]
    q = np.arange(512)[None, :]
    for g in range(2):
        for j in range(8):
            kc = j if g == 0 else 8 + j
            valid = (kc * 128 + kk) <= (qstarts[g] + q)
            m[:, (g * 8 + j) * 512:(g * 8 + j + 1) * 512] = \
                valid.astype(np.float32)
    return m


def _in_maps(x, shared):
    masks = [_masks_for(0), _masks_for(1)]
    in_maps, own_all = [], []
    for c in range(8):
        b, sub = c // 2, c % 2
        qs = (0, 1536) if sub == 0 else (512, 1024)
        own = np.r_[qs[0]:qs[0] + 512, qs[1]:qs[1] + 512]
        own_all.append(own)
        xb = x[b]
        im = dict(shared)
        im["xf_g"] = np.ascontiguousarray(xb.T)
        im["xtm_g"] = np.ascontiguousarray(xb)
        im["xf_o"] = np.ascontiguousarray(xb[own].T)
        im["xtm_o"] = np.ascontiguousarray(xb[own])
        im["masks"] = masks[sub]
        in_maps.append(im)
    return in_maps, own_all


def kernel(L0_attn_x, L0_attn_W_qkv, L0_attn_b_qkv, L0_attn_W_proj,
           L0_attn_b_proj, L0_ffn_gate_w, L0_ffn_up_w, L0_ffn_down_w,
           L0_ffn_gate_s, L0_ffn_up_s, L0_ffn_down_s):
    f32 = np.float32
    x = np.ascontiguousarray(np.asarray(L0_attn_x, f32))
    shared = _prep_shared(L0_attn_W_qkv, L0_attn_b_qkv, L0_attn_W_proj,
                          L0_attn_b_proj, L0_ffn_gate_w, L0_ffn_up_w,
                          L0_ffn_down_w, L0_ffn_gate_s, L0_ffn_up_s,
                          L0_ffn_down_s)
    nc = build()
    in_maps, own_all = _in_maps(x, shared)
    res = run_bass_kernel_spmd(nc, in_maps, core_ids=list(range(8)))
    out = np.empty((B, S, H), dtype=f32)
    for c in range(8):
        out[c // 2][own_all[c]] = np.asarray(res.results[c]["out"])
    return out


# version 29 (final)
# speedup vs baseline: 1.4383x; 1.0342x over previous
import numpy as np
from contextlib import ExitStack

import concourse.bass as bass
import concourse.mybir as mybir
from concourse.bass_utils import run_bass_kernel_spmd

F32 = mybir.dt.float32
BF16 = mybir.dt.bfloat16
AF = mybir.ActivationFunctionType
ALU = mybir.AluOpType
AX = mybir.AxisListType

B, S, H, NH, HD, INTER = 4, 2048, 2048, 16, 128, 5632
SCALE = float(HD) ** -0.5
EPS = 1e-5
HC = H // 128
IC = INTER // 128
NKC = (8, 16)
R_P = 4

A_XN = 0
A_Q = 16384
A_AT = 32768
A_MSK = 49152
A_HDN = 32768
A_K = 57344
A_V = 59392
A_PR = 61440
A_BC = 63488
ARENA_E = 67584

ENGS = ("sync", "gpsimd", "tensor", "scalar", "vector")



def build(debug=False, dbg_tap=None):
    nc = bass.Bass("TRN2", target_bir_lowering=False)

    d_xf_g = nc.dram_tensor("xf_g", [H, S], F32, kind="ExternalInput").ap()
    d_xtm_g = nc.dram_tensor("xtm_g", [S, H], F32, kind="ExternalInput").ap()
    d_xf_o = nc.dram_tensor("xf_o", [H, 1024], F32, kind="ExternalInput").ap()
    d_xtm_o = nc.dram_tensor("xtm_o", [1024, H], F32, kind="ExternalInput").ap()
    d_wq = nc.dram_tensor("wq_t", [18, 128, 2048], F32, kind="ExternalInput").ap()
    d_bqkv = nc.dram_tensor("bqkv_t", [128, 18], F32, kind="ExternalInput").ap()
    d_wp = nc.dram_tensor("wp_t", [16, 128, 2048], F32, kind="ExternalInput").ap()
    d_bproj = nc.dram_tensor("bproj_b", [128, 2048], F32, kind="ExternalInput").ap()
    d_gu = nc.dram_tensor("gu_t", [44, 128, 4096], F32, kind="ExternalInput").ap()
    d_gs = nc.dram_tensor("gs_t", [128, 44], F32, kind="ExternalInput").ap()
    d_us = nc.dram_tensor("us_t", [128, 44], F32, kind="ExternalInput").ap()
    d_dn = nc.dram_tensor("dn_t", [2, 44, 128, 1024], F32, kind="ExternalInput").ap()
    d_dsb = nc.dram_tensor("dsb_b", [128, 2048], F32, kind="ExternalInput").ap()
    d_msk = nc.dram_tensor("masks", [128, 8192], F32, kind="ExternalInput").ap()
    d_id = nc.dram_tensor("ident", [128, 128], F32, kind="ExternalInput").ap()
    d_out = nc.dram_tensor("out", [1024, H], F32, kind="ExternalOutput").ap()
    dbg_t = {}
    if debug:
        for nm, width in [("dbg_xn", 16384), ("dbg_q", 16384), ("dbg_k", 2048),
                          ("dbg_v", 2048), ("dbg_at", 16384), ("dbg_s", 16384),
                          ("dbg_hd", 22528)]:
            dbg_t[nm] = nc.dram_tensor(nm, [128, width], BF16,
                                       kind="ExternalOutput").ap()

    es = ExitStack()
    sb = lambda nm, shp, dt: es.enter_context(nc.sbuf_tensor(nm, shp, dt))
    pt = lambda nm, shp, dt: es.enter_context(nc.psum_tensor(nm, shp, dt))

    arena = sb("arena", [128, ARENA_E], BF16)
    wring = sb("wring", [128, 8192], BF16)
    wd = sb("wd", [128, 3072], BF16)
    xtm = sb("xtm", [128, 2 * 2048], F32)
    xf = sb("xf", [128, 4 * 512], F32)
    fscr = sb("fscr", [128, 8 * 512], F32)
    outt = sb("outt", [128, 2 * 512], F32)
    st24 = outt[:, 0:96]
    stat = outt[:, 96:224]
    rrow = sb("rrow", [1, 2048], F32)
    bqkv = sb("bqkv", [128, 18], F32)
    gsc = sb("gsc", [128, 44], F32)
    usc = sb("usc", [128, 44], F32)
    id32 = sb("id32", [128, 128], F32)
    ones_kk = sb("ones_kk", [128, 1], BF16)
    ones_b32 = sb("ones_b32", [1, 128], F32)

    ps = [pt(f"ps{i}", [128, 512], F32) for i in range(8)]

    sems = {nm: es.enter_context(nc.semaphore(f"s_{nm}"))
            for nm in ("tM", "aM", "vM", "pM", "cC", "cG",
                       "cXT0", "cXT1", "cXF0", "cXF1", "cXF2", "cXF3",
                       "cW0", "cW1", "cD0", "cD1", "cD2", "oS0", "oS1",
                       "cK0", "cK1")}
    cnt = {nm: 0 for nm in sems}
    EV = {e: [] for e in ENGS}

    def ev(eng, fn, waits=(), inc=None, k=1):
        if inc is not None:
            cnt[inc] += k
        EV[eng].append((tuple((s, w) for s, w in waits if w and w > 0),
                        fn, inc, k))
        return cnt[inc] if inc else None

    def dma(eng, out, in_, waits=(), inc=None):
        return ev(eng, lambda e, o=out, i=in_: e.dma_start(out=o, in_=i),
                  waits=waits, inc=inc, k=16)

    def fs(i):
        return fscr[:, i * 512:(i + 1) * 512]

    ev("gpsimd", lambda e: e.memset(ones_kk[:, :], 1.0), inc="pM")
    ev("gpsimd", lambda e: e.memset(ones_b32[:, :], 1.0), inc="pM")
    w_ones = cnt["pM"]
    for dst, src in [(bqkv, d_bqkv), (gsc, d_gs), (usc, d_us), (id32, d_id)]:
        dma("sync", dst[:, :], src[:, :], inc="cC")
    w_cst = ("cC", cnt["cC"])

    xtm_i = [0]
    xf_i = [0]
    xtm_free = {}
    xf_free = {}
    tg_i = [0]
    LN_DVE_END = {}
    TR_LAST = {}
    ROW_FREE = {}
    RB_FREE = {}
    LN_RB = {}
    LN_BC = {}
    LN_NORM_END = {}

    def emit_ln_stats(src_tm, tm_row0):
        tg = tg_i[0]
        tg_i[0] += 1
        par = tg % 2
        Q0 = 0
        P0 = 32 + par * 48
        st = stat
        first_stats = True
        for ccj in range(4):
            ci = xtm_i[0]
            xtm_i[0] += 1
            sp = ci % 2
            tmslot = xtm[:, sp * 2048:(sp + 1) * 2048]
            wts = [w_cst]
            if (ci - 2) in xtm_free:
                wts.append(("vM", xtm_free[ci - 2]))
            xtsem = f"cXT{sp}"
            dma("sync", tmslot,
                src_tm[tm_row0 + ccj * 128: tm_row0 + (ccj + 1) * 128, :],
                waits=wts, inc=xtsem)
            ld = (xtsem, cnt[xtsem])
            stw = [ld]
            if first_stats:
                first_stats = False
                if (tg - 1) in LN_DVE_END:
                    stw.append(("vM", LN_DVE_END[tg - 1]))
            for j in range(4):
                ev("vector",
                   lambda e, o=st24[:, Q0 + ccj * 24 + 6 * j:
                                    Q0 + ccj * 24 + 6 * (j + 1)],
                   i=tmslot[:, j * 512:(j + 1) * 512]: e.bn_stats(o, i),
                   waits=stw if j == 0 else (),
                   inc="vM" if j == 3 else None)
            xtm_free[ci] = cnt["vM"]
        def vstep(fn, extra=()):
            pv = cnt["vM"]
            ev("vector", fn, waits=[("vM", pv)] + list(extra), inc="vM")
        means32 = st24[:, Q0 + 1:Q0 + 96:3]
        m2s32 = st24[:, Q0 + 2:Q0 + 96:3]
        means3 = means32.rearrange("p (c m) -> p c m", m=8)
        m2s3 = m2s32.rearrange("p (c m) -> p c m", m=8)
        msq = stat[:, 0:32]
        msq3 = msq.rearrange("p (c m) -> p c m", m=8)
        c_summ = st[:, P0 + 0:P0 + 4]
        c_sumM2 = st[:, P0 + 4:P0 + 8]
        c_summsq = st[:, P0 + 8:P0 + 12]
        c_m = st[:, P0 + 12:P0 + 16]
        c_ex2a = st[:, P0 + 16:P0 + 20]
        c_t5 = st[:, P0 + 20:P0 + 24]
        c_ex2 = st[:, P0 + 24:P0 + 28]
        c_msq2 = st[:, P0 + 28:P0 + 32]
        c_var = st[:, P0 + 32:P0 + 36]
        c_lnv = st[:, P0 + 36:P0 + 40]
        c_rstd = st[:, P0 + 40:P0 + 44]
        c_mr = st[:, P0 + 44:P0 + 48]
        vstep(lambda e, o=c_summ, i=means3: e.reduce_sum(o, i, axis=AX.X),
              extra=[("tM", TR_LAST[tg - 2])] if (tg - 2) in TR_LAST else ())
        vstep(lambda e, o=c_sumM2, i=m2s3: e.reduce_sum(o, i, axis=AX.X))
        vstep(lambda e, o=msq, a=means32, b=means32:
              e.tensor_tensor(o, a, b, ALU.mult))
        vstep(lambda e, o=c_summsq, i=msq3: e.reduce_sum(o, i, axis=AX.X))
        vstep(lambda e, o=c_m, i=c_summ:
              e.tensor_scalar(o, i, 0.125, None, ALU.mult))
        vstep(lambda e, o=c_ex2a, i=c_sumM2:
              e.tensor_scalar(o, i, 1.0 / 2048.0, None, ALU.mult))
        vstep(lambda e, o=c_t5, i=c_summsq:
              e.tensor_scalar(o, i, 0.125, None, ALU.mult))
        vstep(lambda e, o=c_ex2, a=c_ex2a, b=c_t5:
              e.tensor_tensor(o, a, b, ALU.add))
        vstep(lambda e, o=c_msq2, a=c_m, b=c_m:
              e.tensor_tensor(o, a, b, ALU.mult))
        vstep(lambda e, o=c_var, a=c_ex2, b=c_msq2:
              e.tensor_tensor(o, a, b, ALU.subtract))
        vstep(lambda e, o=c_lnv, i=c_var:
              e.tensor_scalar(o, i, EPS, None, ALU.add))
        var_done = cnt["vM"]
        lnw = [("vM", var_done)]
        if (tg - 2) in TR_LAST:
            lnw.append(("tM", TR_LAST[tg - 2]))
        ev("scalar", lambda e, o=c_lnv, i=c_lnv:
           e.activation(o, i, AF.Ln), waits=lnw, inc="aM")
        _lv = cnt["aM"]
        ev("scalar", lambda e, o=c_rstd, i=c_lnv:
           e.activation(o, i, AF.Exp, scale=-0.5),
           waits=[("aM", _lv)], inc="aM")
        rstd_done = cnt["aM"]
        vstep(lambda e, o=c_mr, a=c_m, b=c_rstd:
              e.tensor_tensor(o, a, b, ALU.mult),
              extra=[("aM", rstd_done)])
        mr_done = cnt["vM"]
        LN_DVE_END[tg] = mr_done
        for ccj in range(4):
            trw = []
            if ccj == 0:
                trw = [("aM", rstd_done), ("vM", mr_done), w_cst]
                if (tg - 2) in ROW_FREE:
                    trw.append(("vM", ROW_FREE[tg - 2]))
                if (tg - 2) in LN_BC:
                    trw.append(("tM", LN_BC[tg - 2]))
            ev("tensor", lambda e, o=ps[2][0:1, ccj * 128:(ccj + 1) * 128],
               i=c_rstd[:, ccj:ccj + 1]: e.transpose(o, i, id32[:, :]),
               waits=trw)
            ev("tensor", lambda e, o=ps[3][0:1, ccj * 128:(ccj + 1) * 128],
               i=c_mr[:, ccj:ccj + 1]: e.transpose(o, i, id32[:, :]),
               inc="tM" if ccj == 3 else None)
        TR_LAST[tg] = cnt["tM"]
        tr_last = cnt["tM"]
        rsl = rrow[0:1, par * 512:(par + 1) * 512]
        mrl = rrow[0:1, 1024 + par * 512:1024 + (par + 1) * 512]
        ev("vector", lambda e, o=rsl, i=ps[2][0:1, 0:512]: e.tensor_copy(o, i),
           waits=[("tM", tr_last)], inc="vM")
        _rv = cnt["vM"]
        ev("vector", lambda e, o=mrl, i=ps[3][0:1, 0:512]: e.tensor_copy(o, i),
           waits=[("vM", _rv)], inc="vM")
        ROW_FREE[tg] = cnt["vM"]
        row_done = cnt["vM"]
        bw = [("vM", row_done), ("pM", w_ones)]
        if (tg - 1) in RB_FREE:
            bw.append(("vM", RB_FREE[tg - 1]))
        ev("tensor", lambda e, o=ps[4][:, :], l=ones_b32[:, :], r=rsl:
           e.matmul(o, l, r, start=True, stop=True), waits=bw)
        ev("tensor", lambda e, o=ps[5][:, :], l=ones_b32[:, :], r=mrl:
           e.matmul(o, l, r, start=True, stop=True), inc="tM")
        LN_BC[tg] = cnt["tM"]
        bc_v = cnt["tM"]
        rb_s, mrb_s = fs(par), fs(2 + par)
        rbw = [("tM", bc_v)]
        if (tg - 2) in LN_NORM_END:
            rbw.append(("vM", LN_NORM_END[tg - 2]))
        ev("vector", lambda e, o=rb_s, i=ps[4][:, :]: e.tensor_copy(o, i),
           waits=rbw, inc="vM")
        _rc = cnt["vM"]
        ev("vector", lambda e, o=mrb_s, i=ps[5][:, :]: e.tensor_copy(o, i),
           waits=[("vM", _rc)], inc="vM")
        RB_FREE[tg] = cnt["vM"]
        rb_done = cnt["vM"]
        LN_RB[tg] = (rb_s, mrb_s, cnt["vM"])
        return tg

    def emit_ln_norm(tg, src_f, f_col0, dst_fn, norm_extra=(),
                     hc_range=None, xn_done=None):
        rb_s, mrb_s, rb_done = LN_RB[tg]
        if xn_done is None:
            xn_done = []
        for hc in (hc_range if hc_range is not None else range(HC)):
            fi = xf_i[0]
            xf_i[0] += 1
            fsl = xf[:, (fi % 4) * 512:(fi % 4 + 1) * 512]
            fw = [w_cst]
            if (fi - 4) in xf_free:
                fw.append(("vM", xf_free[fi - 4]))
            xfsem = f"cXF{fi % 4}"
            dma("sync", fsl, src_f[hc * 128:(hc + 1) * 128, f_col0:f_col0 + 512],
                waits=fw, inc=xfsem)
            fl = (xfsem, cnt[xfsem])
            t1 = fs(4 + hc % 2)
            t1w = [fl, ("vM", rb_done)]
            if hc >= 2:
                t1w.append(("vM", xn_done[hc - 2]))
            ev("vector", lambda e, o=t1, a=fsl, b=rb_s:
               e.tensor_tensor(o, a, b, ALU.mult),
               waits=t1w + (list(norm_extra) if hc == 0 else []), inc="vM")
            t1v_ = cnt["vM"]
            ev("vector", lambda e, o=dst_fn(hc), a=t1, b=mrb_s:
               e.tensor_tensor(o, a, b, ALU.subtract),
               waits=[("vM", t1v_)], inc="vM")
            xf_free[fi] = cnt["vM"]
            xn_done.append(cnt["vM"])
        if len(xn_done) == HC:
            LN_NORM_END[tg] = cnt["vM"]
        return xn_done

    xn_ready = {}

    def emit_norm_own(g):
        vals = emit_ln_norm(
            g, d_xf_o, g * 512,
            lambda hc, g=g: arena[:, A_XN + hc * 1024 + g * 512:
                                  A_XN + hc * 1024 + g * 512 + 512])
        for hc in range(HC):
            xn_ready[(hc, g)] = vals[hc]

    emit_ln_stats(d_xtm_o, 0)
    emit_ln_stats(d_xtm_o, 512)
    emit_norm_own(0)

    dma("gpsimd", wring[:, 0:2048], d_wq[16], waits=[("cXT1", 32)], inc="cK0")
    kv_k_ld = ("cK0", cnt["cK0"])
    dma("gpsimd", wring[:, 2048:4096], d_wq[17], inc="cK1")
    kv_v_ld = ("cK1", cnt["cK1"])

    wq_load = {}
    wq_free = {}
    q_done = {}
    k_done = {}
    v_done = {}
    kv_prev = {"K": None, "V": None}
    vstage_free = {}

    def emit_q_chunk(n):
        slot = wring[:, 4096 + (n % 2) * 2048: 4096 + (n % 2 + 1) * 2048]
        lw = []
        if (n - 2) in wq_free:
            lw.append(("tM", wq_free[n - 2]))
        wsem = f"cW{n % 2}"
        dma("gpsimd", slot, d_wq[n], waits=lw, inc=wsem)
        wq_load[n] = (wsem, cnt[wsem])
        for g in range(2):
            bank = ps[(n * 2 + g) % 2]
            xn_all_g = max(xn_ready[(hc, g)] for hc in range(HC))
            for hc in range(HC):
                if hc == 0:
                    wts = [wq_load[n], ("vM", xn_all_g)]
                    if (n - 1, g) in q_done:
                        wts.append(("aM", q_done[(n - 1, g)]))
                else:
                    wts = ()
                ev("tensor", lambda e, o=bank[:, :],
                   l=slot[:, hc * 128:(hc + 1) * 128],
                   r=arena[:, A_XN + hc * 1024 + g * 512:
                           A_XN + hc * 1024 + g * 512 + 512],
                   st=(hc == 0), sp=(hc == HC - 1):
                   e.matmul(o, l, r, start=st, stop=sp),
                   waits=wts, inc="tM" if hc == HC - 1 else None)
            mm_done = cnt["tM"]
            if g == 1:
                wq_free[n] = mm_done
            ev("scalar", lambda e,
               o=arena[:, A_Q + n * 1024 + g * 512:
                       A_Q + n * 1024 + g * 512 + 512],
               i=bank[:, :], b=bqkv[:, n:n + 1]:
               e.activation(o, i, AF.Identity, bias=b),
               waits=[("tM", mm_done), w_cst], inc="aM")
            q_done[(n, g)] = cnt["aM"]

    def emit_kv_tg(tg, xn_vals, xn_fn):
        for nn, nm in ((16, "K"), (17, "V")):
            bank = ps[6] if nm == "K" else ps[7]
            xn_all = max(xn_vals)
            for hc in range(HC):
                wts = ()
                if hc == 0:
                    wts = [("vM", xn_all),
                           kv_k_ld if nm == "K" else kv_v_ld]
                    if kv_prev[nm] is not None:
                        wts.append(kv_prev[nm])
                    if tg == 2:
                        if nm == "K":
                            wts.append(("aM", ex_done[NB_A - 1]))
                        else:
                            wts.append(("vM", rbA_done[13]))
                    elif tg == 3:
                        if nm == "K":
                            wts.append(("aM", ex_done[NB_A + 7]))
                        else:
                            wts.append(("vM", rbA_done[14]))
                ev("tensor", lambda e, o=bank[:, :],
                   l=wring[:, (nn - 16) * 2048 + hc * 128:
                           (nn - 16) * 2048 + (hc + 1) * 128],
                   r=xn_fn(hc), st=(hc == 0), sp=(hc == HC - 1):
                   e.matmul(o, l, r, start=st, stop=sp),
                   waits=wts, inc="tM" if hc == HC - 1 else None)
            mm_done = cnt["tM"]
            if nm == "K":
                ev("scalar", lambda e,
                   o=arena[:, A_K + tg * 512:A_K + (tg + 1) * 512],
                   i=bank[:, :], b=bqkv[:, 16:17]:
                   e.activation(o, i, AF.Identity, bias=b),
                   waits=[("tM", mm_done), w_cst], inc="aM")
                k_done[tg] = cnt["aM"]
                kv_prev["K"] = ("aM", cnt["aM"])
            else:
                vst = fs(6)
                vw = [("tM", mm_done), w_cst]
                if (tg - 1) in vstage_free:
                    vw.append(("tM", vstage_free[tg - 1]))
                ev("scalar", lambda e, o=vst, i=bank[:, :], b=bqkv[:, 17:18]:
                   e.activation(o, i, AF.Identity, bias=b),
                   waits=vw, inc="aM")
                vev = cnt["aM"]
                for j in range(4):
                    ev("tensor", lambda e, o=ps[7][:, j * 128:(j + 1) * 128],
                       i=vst[:, j * 128:(j + 1) * 128]:
                       e.transpose(o, i, id32[:, :]),
                       waits=[("aM", vev)] if j == 0 else (),
                       inc="tM" if j == 3 else None)
                vstage_free[tg] = cnt["tM"]
                tr_done = cnt["tM"]
                ev("vector", lambda e,
                   o=arena[:, A_V + tg * 512:A_V + (tg + 1) * 512],
                   i=ps[7][:, 0:512]: e.tensor_copy(o, i),
                   waits=[("tM", tr_done)], inc="vM")
                v_done[tg] = cnt["vM"]
                kv_prev["V"] = ("vM", cnt["vM"])

    def emit_key_norm_kv(ktg):
        tgid = 2 + ktg
        base = 0 if ktg == 0 else 8192
        extra = [("tM", vstage_free[ktg - 1])] if ktg >= 2 else []
        xn_fn = lambda hc, base=base: arena[:, A_AT + base + hc * 512:
                                            A_AT + base + (hc + 1) * 512]
        vals = emit_ln_norm(tgid, d_xf_g, ktg * 512, xn_fn, norm_extra=extra)
        emit_kv_tg(ktg, vals, xn_fn)

    emit_ln_stats(d_xtm_g, 0)
    emit_norm_own(1)
    for n in range(0, 8):
        emit_q_chunk(n)
    emit_ln_stats(d_xtm_g, 512)
    emit_key_norm_kv(0)
    for n in range(8, 12):
        emit_q_chunk(n)
    dma("gpsimd", arena[:, A_MSK:A_MSK + 8192], d_msk[:, :], inc="cG")
    dma("gpsimd", arena[:, A_BC:A_BC + 2048], d_bproj[:, :], inc="cG")
    dma("gpsimd", arena[:, A_BC + 2048:A_BC + 4096], d_dsb[:, :], inc="cG")
    w_msk = w_bproj = w_dsb = ("cG", cnt["cG"])

    emit_ln_stats(d_xtm_g, 1024)
    emit_key_norm_kv(1)
    for n in range(12, 16):
        emit_q_chunk(n)
    emit_ln_stats(d_xtm_g, 1536)

    groups = [(h, 0) for h in range(NH)] + [(h, 1) for h in range(NH)]
    blocks = []
    for hi, (h, g) in enumerate(groups):
        for kc in range(NKC[g]):
            blocks.append((hi, h, g, kc))
    NB = len(blocks)

    sc_done = [0] * NB
    ex_done = [0] * NB
    pd_done = [0] * NB
    pd_sem = [None] * NB
    pv_done = [0] * NB
    rc_done = {}
    bcA_done = {}
    rbA_done = {}
    at_done = {}
    grp_sums_stop = {}
    grp_pv_stop = {}

    q_last = q_done[(15, 1)]
    rowfree_last = ROW_FREE[tg_i[0] - 1]
    rbfree_last = RB_FREE[tg_i[0] - 1]

    def blk_masked(g, kc):
        return g == 0 or kc >= 8

    def mask_slot(g, kc):
        return g * 8 + (kc if g == 0 else kc - 8)

    SC_BANKS = (ps[0], ps[1], ps[6])

    def emit_scores(t):
        hi, h, g, kc = blocks[t]
        bank = SC_BANKS[t % 3]
        wts = []
        if kc == 0:
            wts.append(("aM", q_done[(h, g)]))
        if kc % 4 == 0:
            wts.append(("aM", k_done[kc // 4]))
        if t >= 3:
            wts.append(("aM", ex_done[t - 3]))
        elif t < 2:
            wts.append(("aM", q_last))
        if NB_A <= t < NB_A + 3:
            wts.append(("aM", k_done[2]))
            wts.append(("vM", v_done[2]))
        if NB_A + 8 <= t < NB_A + 11:
            wts.append(("aM", k_done[3]))
            wts.append(("vM", v_done[3]))
        ev("tensor", lambda e, o=bank[:, :],
           l=arena[:, A_K + kc * 128:A_K + (kc + 1) * 128],
           r=arena[:, A_Q + h * 1024 + g * 512:A_Q + h * 1024 + (g + 1) * 512]:
           e.matmul(o, l, r, start=True, stop=True),
           waits=wts, inc="tM")
        sc_done[t] = cnt["tM"]

    def emit_exp(t):
        hi, h, g, kc = blocks[t]
        bank = SC_BANKS[t % 3]
        slot = arena[:, A_PR + (t % R_P) * 512:A_PR + (t % R_P + 1) * 512]
        wts = [("tM", sc_done[t])]
        if t >= R_P:
            wts.append(("tM", pv_done[t - R_P]))
        ev("scalar", lambda e, o=slot, i=bank[:, :]:
           e.activation(o, i, AF.Exp, scale=SCALE),
           waits=wts, inc="aM")
        ex_done[t] = cnt["aM"]
        if blk_masked(g, kc):
            ms = mask_slot(g, kc)
            ev("vector", lambda e, o=slot, a=slot,
               b=arena[:, A_MSK + ms * 512:A_MSK + (ms + 1) * 512]:
               e.tensor_tensor(o, a, b, ALU.mult),
               waits=[("aM", ex_done[t]), w_msk], inc="vM")
            pd_done[t], pd_sem[t] = cnt["vM"], "vM"
        else:
            pd_done[t], pd_sem[t] = ex_done[t], "aM"

    def emit_sums_pv(t):
        hi, h, g, kc = blocks[t]
        slot = arena[:, A_PR + (t % R_P) * 512:A_PR + (t % R_P + 1) * 512]
        sums_bank = ps[4 + hi % 2]
        pv_bank = ps[2 + hi % 2]
        first, last = kc == 0, kc == NKC[g] - 1
        wts = [(pd_sem[t], pd_done[t])]
        if first:
            wts.append(("pM", w_ones))
            if (hi - 2) in rc_done:
                wts.append(rc_done[hi - 2])
            elif hi < 2:
                wts.append(("vM", rbfree_last))
        ev("tensor", lambda e, o=sums_bank[0:1, :], l=ones_kk[:, :], r=slot,
           st=first, sp=last: e.matmul(o, l, r, start=st, stop=sp),
           waits=wts, inc="tM" if last else None)
        if last:
            grp_sums_stop[hi] = cnt["tM"]
        pw = []
        if kc % 4 == 0:
            pw.append(("vM", v_done[kc // 4]))
        if first:
            if (hi - 2) in at_done:
                pw.append(("vM", at_done[hi - 2]))
            elif hi < 2:
                pw.append(("vM", rowfree_last))
        ev("tensor", lambda e, o=pv_bank[:, :],
           l=arena[:, A_V + kc * 128:A_V + (kc + 1) * 128],
           r=slot, st=first, sp=last:
           e.matmul(o, l, r, start=st, stop=sp),
           waits=pw, inc="tM")
        pv_done[t] = cnt["tM"]
        if last:
            grp_pv_stop[hi] = cnt["tM"]

    def emit_tail_dve(hi):
        par = hi % 2
        rsl = outt[0:1, par * 512:(par + 1) * 512]
        lsl = outt[32:33, par * 512:(par + 1) * 512]
        ev("scalar", lambda e, o=lsl, i=ps[4 + par][0:1, :]:
           e.activation(o, i, AF.Ln),
           waits=[("tM", grp_sums_stop[hi])], inc="aM")
        _lv = cnt["aM"]
        ev("scalar", lambda e, o=rsl, i=lsl:
           e.activation(o, i, AF.Exp, scale=-1.0),
           waits=[("aM", _lv)], inc="aM")
        rc_done[hi] = ("aM", cnt["aM"])

    def emit_tail_bcast(hi):
        par = hi % 2
        rsl = outt[0:1, par * 512:(par + 1) * 512]
        wts = [rc_done[hi]]
        if (hi - 1) in rbA_done:
            wts.append(("vM", rbA_done[hi - 1]))
        if hi == 0:
            wts.append(("tM", vstage_free[1]))
        if hi == 14:
            wts.append(("tM", vstage_free[2]))
            wts.append(("vM", v_done[2]))
        if hi in (15, 16):
            wts.append(("tM", vstage_free[3]))
            wts.append(("vM", v_done[3]))
        ev("tensor", lambda e, o=ps[7][:, :], l=ones_b32[:, :], r=rsl:
           e.matmul(o, l, r, start=True, stop=True),
           waits=wts, inc="tM")
        bcA_done[hi] = cnt["tM"]

    def emit_tail_evac(hi):
        par = hi % 2
        h, g = groups[hi]
        rbs = xtm[:, par * 512:(par + 1) * 512]
        rcw = [("tM", bcA_done[hi])]
        if (hi - 2) in at_done:
            rcw.append(("vM", at_done[hi - 2]))
        ev("vector", lambda e, o=rbs, i=ps[7][:, :]: e.tensor_copy(o, i),
           waits=rcw, inc="vM")
        rbA_done[hi] = cnt["vM"]
        ev("vector", lambda e,
           o=arena[:, A_AT + g * 8192 + h * 512:
                   A_AT + g * 8192 + (h + 1) * 512],
           a=ps[2 + par][:, :], b=rbs:
           e.tensor_tensor(o, a, b, ALU.mult),
           waits=[("tM", grp_pv_stop[hi]), ("vM", rbA_done[hi])], inc="vM")
        at_done[hi] = cnt["vM"]

    NB_A = sum(NKC[0] for _ in range(NH))
    key_vals = {}
    key_xfn = {}

    def emit_key_norm_only(ktg):
        tgid = 2 + ktg
        base = 0 if ktg == 0 else 8192
        extra = [("tM", vstage_free[ktg - 1])] if ktg >= 2 else []
        xn_fn = lambda hc, base=base: arena[:, A_AT + base + hc * 512:
                                            A_AT + base + (hc + 1) * 512]
        key_vals[ktg] = emit_ln_norm(tgid, d_xf_g, ktg * 512, xn_fn,
                                     norm_extra=extra)
        key_xfn[ktg] = xn_fn

    def emit_key_norm_piece(ktg, piece):
        tgid = 2 + ktg
        base = 0 if ktg == 0 else 8192
        extra = ([("tM", vstage_free[ktg - 1])]
                 if (ktg >= 2 and piece == 0) else [])
        xn_fn = lambda hc, base=base: arena[:, A_AT + base + hc * 512:
                                            A_AT + base + (hc + 1) * 512]
        key_xfn[ktg] = xn_fn
        acc = key_vals.setdefault(ktg, [])
        emit_ln_norm(tgid, d_xf_g, ktg * 512, xn_fn, norm_extra=extra,
                     hc_range=range(piece * 4, (piece + 1) * 4), xn_done=acc)

    pend = None
    for t in range(NB):
        hi, h, g, kc = blocks[t]
        if NB_A - 40 <= t < NB_A - 40 + 32 and (t - (NB_A - 40)) % 8 == 0:
            emit_key_norm_piece(2, (t - (NB_A - 40)) // 8)
        if t == NB_A:
            emit_kv_tg(2, key_vals[2], key_xfn[2])
        if NB_A <= t < NB_A + 8 and (t - NB_A) % 2 == 0:
            emit_key_norm_piece(3, (t - NB_A) // 2)
        if t == NB_A + 8:
            emit_kv_tg(3, key_vals[3], key_xfn[3])
        emit_scores(t)
        if kc == 1 and pend is not None:
            emit_tail_bcast(pend)
            emit_tail_evac(pend)
            pend = None
        emit_exp(t)
        if t >= 2:
            emit_sums_pv(t - 2)
            phi, _, pg, pkc = blocks[t - 2]
            if pkc == NKC[pg] - 1:
                emit_tail_dve(phi)
                pend = phi
    for tt in (NB - 2, NB - 1):
        emit_sums_pv(tt)
        phi, _, pg, pkc = blocks[tt]
        if pkc == NKC[pg] - 1:
            emit_tail_dve(phi)
            if pend is not None and pend != phi:
                emit_tail_bcast(pend)
                emit_tail_evac(pend)
            pend = phi
    if pend is not None and pend != len(groups) - 1:
        emit_tail_bcast(pend)
        emit_tail_evac(pend)
    emit_tail_bcast(len(groups) - 1)
    emit_tail_evac(len(groups) - 1)

    wp_load = {}
    wp_free = {}
    s_done = {}
    pj_free = {}
    attn_all = max(at_done[len(groups) - 1], rbA_done[len(groups) - 1])
    exp_all = max(ex_done[NB - 1], rc_done[len(groups) - 1][1])

    for sw in range(4):
        fgh, qq = sw // 2, sw % 2
        for a in range(16):
            ap_ = a // 2
            slot = wring[:, (ap_ % 4) * 2048:(ap_ % 4 + 1) * 2048]
            if a % 2 == 0:
                lw = []
                prev = (sw, a - 7) if a >= 8 else \
                       ((sw - 1, 9 + a) if sw > 0 else None)
                if prev in wp_free:
                    lw.append(("tM", wp_free[prev]))
                if sw == 0 and a < 8:
                    lw.append(("tM", vstage_free[3]))
                wsem = ("cW0", "cW1", "cK0", "cK1")[ap_ % 4]
                dma("gpsimd",
                    slot.rearrange("p (i f) -> p i f", f=1024),
                    d_wp[a:a + 2][:, :, fgh * 1024:(fgh + 1) * 1024]
                    .transpose([1, 0, 2]),
                    waits=lw, inc=wsem)
                wp_load[(sw, a)] = (wsem, cnt[wsem])
                wp_load[(sw, a + 1)] = (wsem, cnt[wsem])
            first_mm = True
            for qc_l in range(4):
                qcg = qq * 4 + qc_l
                g = qcg // 4
                for fgi in range(2):
                    bank = ps[qc_l * 2 + fgi]
                    wts = ()
                    if first_mm:
                        first_mm = False
                        wts = [wp_load[(sw, a)],
                               ("vM", max(at_done[groups.index((a, 0))], at_done[groups.index((a, 1))]))]
                        if a == 0:
                            if sw >= 1:
                                wts.append(("vM", max(
                                    pj_free[(sw - 1, b)] for b in range(8))))
                            else:
                                wts.append(("aM", exp_all))
                                wts.append(("vM", attn_all))
                    ev("tensor", lambda e, o=bank[:, :],
                       l=arena[:, A_AT + g * 8192 + a * 512 + (qcg % 4) * 128:
                               A_AT + g * 8192 + a * 512 + (qcg % 4 + 1) * 128],
                       r=slot[:, (a % 2) * 1024 + fgi * 512:
                              (a % 2) * 1024 + (fgi + 1) * 512],
                       st=(a == 0), sp=(a == 15):
                       e.matmul(o, l, r, start=st, stop=sp),
                       waits=wts,
                       inc="tM" if (qc_l == 3 and fgi == 1) else None)
            wp_free[(sw, a)] = cnt["tM"]
        sweep_done = cnt["tM"]
        for qc_l in range(4):
            qcg = qq * 4 + qc_l
            for fgi in range(2):
                fg = fgh * 2 + fgi
                bank = ps[qc_l * 2 + fgi]
                ev("vector", lambda e,
                   o=arena[:, A_Q + qcg * 2048 + fg * 512:
                           A_Q + qcg * 2048 + (fg + 1) * 512],
                   a=bank[:, :],
                   b=arena[:, A_BC + fg * 512:A_BC + (fg + 1) * 512]:
                   e.tensor_tensor(o, a, b, ALU.add),
                   waits=[("tM", sweep_done), w_bproj], inc="vM")
                s_done[(qcg, fg)] = cnt["vM"]
                pj_free[(sw, qc_l * 2 + fgi)] = cnt["vM"]
    pj_all = max(pj_free.values())

    gu_load = {}
    gu_free = {}
    hdn_done = {}
    attn_dve_end = attn_all
    dn_load = {}
    dn_free = {}
    dn_slot_prev = {}
    epi_t1 = {}
    epi_gv = {}
    epi_t2v = {}
    epi_outv = {}
    gv_last = {}
    out_last = {}
    xtile_free = {}
    store_v = {}
    out_i = [0]
    last_down = [0]

    for g in range(2):
        sg_free = {}
        us_free = {}
        for i in range(IC):
            slot = wring[:, (i % 2) * 4096:(i % 2 + 1) * 4096]
            key = (g, i)
            ld2 = []
            for half in range(2):
                prev = (g, i - 2) if i >= 2 else \
                       ((g - 1, IC - 2 + i) if g > 0 else None)
                lw = []
                if half == 0:
                    if prev in gu_free:
                        lw.append(("tM", gu_free[prev]))
                    if g == 0 and i < 2:
                        lw.append(("tM", wp_free[(3, 15)]))
                wsem = f"cW{(2 * i + half) % 4}" if False else f"cW{i % 2}"
                hsem = ("cW0", "cW1", "cK0", "cK1")[(2 * i + half) % 4]
                dma("gpsimd", slot[:, half * 2048:(half + 1) * 2048],
                    d_gu[i][:, half * 2048:(half + 1) * 2048],
                    waits=lw, inc=hsem)
                ld2.append((hsem, cnt[hsem]))
            gu_load[key] = ld2
            gu_load2 = ld2
            gb = ps[0 + (i % 2) * 2]
            ub = ps[1 + (i % 2) * 2]
            for hc in range(HC):
                wts = ()
                if hc == 0:
                    wts = [gu_load[key][0]]
                    if prev in sg_free:
                        wts.append(("aM", sg_free[prev]))
                    if g == 0 and i < 2:
                        wts.append(("vM", pj_all))
                    if g == 1 and i == 0:
                        wts.append(("tM", last_down[0]))
                        wts.append(("vM", epi_t1[(0, 3, 3)]))
                ev("tensor", lambda e, o=gb[:, :],
                   l=slot[:, hc * 128:(hc + 1) * 128],
                   r=arena[:, A_XN + hc * 1024 + g * 512:
                           A_XN + hc * 1024 + g * 512 + 512],
                   st=(hc == 0), sp=(hc == HC - 1):
                   e.matmul(o, l, r, start=st, stop=sp),
                   waits=wts, inc="tM" if hc == HC - 1 else None)
            g_stop = cnt["tM"]
            for hc in range(HC):
                wts = ()
                if hc == 0:
                    wts = [gu_load[key][1]]
                    if prev in us_free:
                        wts.append(("vM", us_free[prev]))
                    if g == 0 and i < 2:
                        wts.append(("vM", pj_all))
                ev("tensor", lambda e, o=ub[:, :],
                   l=slot[:, 2048 + hc * 128:2048 + (hc + 1) * 128],
                   r=arena[:, A_XN + hc * 1024 + g * 512:
                           A_XN + hc * 1024 + g * 512 + 512],
                   st=(hc == 0), sp=(hc == HC - 1):
                   e.matmul(o, l, r, start=st, stop=sp),
                   waits=wts, inc="tM" if hc == HC - 1 else None)
            u_stop = cnt["tM"]
            gu_free[key] = cnt["tM"]
            sgs = fs(4 + i % 2)
            uss = fs(6 + i % 2)
            sgw = [("tM", g_stop), w_cst]
            if prev in hdn_done:
                sgw.append(("vM", hdn_done[prev]))
            ev("scalar", lambda e, o=sgs, i_=gb[:, :], sc=gsc[:, i:i + 1]:
               e.activation(o, i_, AF.Silu, scale=sc),
               waits=sgw, inc="aM")
            sg_free[key] = cnt["aM"]
            sg_v = cnt["aM"]
            usw = [("tM", u_stop), w_cst]
            if prev in hdn_done:
                usw.append(("vM", hdn_done[prev]))
            ev("vector", lambda e, o=uss, i_=ub[:, :], sc=usc[:, i:i + 1]:
               e.tensor_scalar(o, i_, sc, None, ALU.mult),
               waits=usw, inc="vM")
            us_free[key] = cnt["vM"]
            hw = [("aM", sg_v), ("vM", us_free[key])]
            if g == 1 and i == 0:
                hw.append(("tM", last_down[0]))
            if g == 0 and i == 32:
                hw.append(("vM", attn_dve_end))
            ev("vector", lambda e,
               o=arena[:, A_HDN + i * 512:A_HDN + (i + 1) * 512],
               a=sgs, b=uss: e.tensor_tensor(o, a, b, ALU.mult),
               waits=hw, inc="vM")
            hdn_done[(g, i)] = cnt["vM"]

        for fgp in range(2):
            def emit_dn_load(i, fgp=fgp, g=g):
                key = (g, fgp, i)
                slot = i % 3
                pkey = dn_slot_prev.get(slot)
                dn_slot_prev[slot] = key
                lw = []
                if pkey is not None:
                    assert pkey in dn_free, f"dn pacing missing {pkey}"
                    lw.append(("tM", dn_free[pkey]))
                dsem = f"cD{slot}"
                dma("gpsimd",
                    wd[:, slot * 1024:(slot + 1) * 1024],
                    d_dn[fgp][i],
                    waits=lw, inc=dsem)
                dn_load[key] = (dsem, cnt[dsem])
            for i in range(3):
                emit_dn_load(i)
            for i in range(IC):
                for qc_l in (2, 3, 0, 1):
                    for fgi in range(2):
                        bank = ps[qc_l * 2 + fgi]
                        wts = []
                        if qc_l == 2 and fgi == 0:
                            wts.append(("vM", hdn_done[(g, i)]))
                            wts.append(dn_load[(g, fgp, i)])
                        if i == 0 and fgi == 0 and qc_l == 0:
                            wts.append(("aM", sg_free[(g, IC - 1)]))
                            wts.append(("vM", us_free[(g, IC - 1)]))
                        if i == 0:
                            fg = fgp * 2 + fgi
                            pk = (g, fg - 2, qc_l) if fgp > 0 else \
                                ((g - 1, fg + 2, qc_l) if g > 0 else None)
                            if pk in epi_t1:
                                wts.append(("vM", epi_t1[pk]))
                            elif g == 0 and fgp == 0 and qc_l == 0 and fgi == 0:
                                wts.append(("vM", pj_all))
                        ev("tensor", lambda e, o=bank[:, :],
                           l=arena[:, A_HDN + i * 512 + qc_l * 128:
                                   A_HDN + i * 512 + (qc_l + 1) * 128],
                           r=wd[:, (i % 3) * 1024 + fgi * 512:
                                (i % 3) * 1024 + (fgi + 1) * 512],
                           st=(i == 0), sp=(i == IC - 1):
                           e.matmul(o, l, r, start=st, stop=sp),
                           waits=wts,
                           inc="tM" if (qc_l == 1 and fgi == 1) else None)
                dn_free[(g, fgp, i)] = cnt["tM"]
                if i + 3 < IC:
                    emit_dn_load(i + 3)
            dn_stop = cnt["tM"]
            last_down[0] = dn_stop
            for fgi in range(2):
                fg = fgp * 2 + fgi
                xls = {}
                for qc_l in range(4):
                    fsl = xf[:, qc_l * 512:(qc_l + 1) * 512]
                    pk = (g, fg - 1, qc_l) if fg > 0 else \
                        ((g - 1, 3, qc_l) if g > 0 else None)
                    xw = []
                    if pk in xtile_free:
                        xw.append(("vM", xtile_free[pk]))
                    elif g == 0 and fg == 0:
                        xw.append(("vM", xf_free[xf_i[0] - 1]))
                    xfsem = f"cXF{qc_l}"
                    dma("sync", fsl,
                        d_xtm_o[g * 512 + qc_l * 128:g * 512 + (qc_l + 1) * 128,
                                fg * 512:(fg + 1) * 512],
                        waits=xw, inc=xfsem)
                    xls[qc_l] = (xfsem, cnt[xfsem])
                for qc_l in (2, 3, 0, 1):
                    bank = ps[qc_l * 2 + fgi]
                    t1 = fs(4 + qc_l % 2)
                    t2 = fs(6 + qc_l % 2)
                    oi = out_i[0]
                    out_i[0] += 1
                    osl = outt[:, (oi % 2) * 512:(oi % 2 + 1) * 512]
                    t1w = [("tM", dn_stop), w_dsb]
                    if qc_l % 2 in gv_last:
                        t1w.append(("aM", gv_last[qc_l % 2]))
                    ev("vector", lambda e, o=t1, a=bank[:, :],
                       b=arena[:, A_BC + 2048 + fg * 512:
                               A_BC + 2048 + (fg + 1) * 512]:
                       e.tensor_tensor(o, a, b, ALU.mult),
                       waits=t1w, inc="vM")
                    t1v = cnt["vM"]
                    epi_t1[(g, fg, qc_l)] = t1v
                    gsl = arena[:, A_PR + qc_l * 512:A_PR + (qc_l + 1) * 512]
                    gvw = [("vM", t1v)]
                    pk3 = (g, fg - 1, qc_l) if fg > 0 else \
                        ((g - 1, 3, qc_l) if g > 0 else None)
                    if pk3 in epi_t2v:
                        gvw.append(("vM", epi_t2v[pk3]))
                    ev("scalar", lambda e, o=gsl, i_=t1:
                       e.activation(o, i_, AF.Silu),
                       waits=gvw, inc="aM")
                    gv = cnt["aM"]
                    epi_gv[(g, fg, qc_l)] = gv
                    gv_last[qc_l % 2] = gv
                    qcg = g * 4 + qc_l
                    ev("vector", lambda e, o=t2, a=gsl,
                       b=arena[:, A_Q + qcg * 2048 + fg * 512:
                               A_Q + qcg * 2048 + (fg + 1) * 512]:
                       e.tensor_tensor(o, a, b, ALU.mult),
                       waits=[("aM", gv), ("vM", s_done[(qcg, fg)])] +
                             ([("vM", out_last[qc_l % 2])]
                              if qc_l % 2 in out_last else []),
                       inc="vM")
                    epi_t2v[(g, fg, qc_l)] = cnt["vM"]
                    ow = [xls[qc_l], ("vM", cnt["vM"])]
                    if oi >= 2:
                        ow.append(store_v[oi - 2])
                    ev("vector", lambda e, o=osl, a=t2,
                       b=xf[:, qc_l * 512:(qc_l + 1) * 512]:
                       e.tensor_tensor(o, a, b, ALU.add),
                       waits=ow, inc="vM")
                    ov = cnt["vM"]
                    epi_outv[(g, fg, qc_l)] = ov
                    out_last[qc_l % 2] = ov
                    xtile_free[(g, fg, qc_l)] = ov
                    osem = f"oS{oi % 2}"
                    dma("sync",
                        d_out[g * 512 + qc_l * 128:g * 512 + (qc_l + 1) * 128,
                              fg * 512:(fg + 1) * 512],
                        osl, waits=[("vM", ov)], inc=osem)
                    store_v[oi] = (osem, cnt[osem])

    if debug:
        allw = [("vM", cnt["vM"]), ("aM", cnt["aM"]), ("tM", cnt["tM"])]
        for nm, off, width in [("dbg_xn", A_XN, 16384), ("dbg_k", A_K, 2048),
                               ("dbg_v", A_V, 2048), ("dbg_at", A_AT, 16384),
                               ("dbg_s", A_Q, 16384), ("dbg_q", A_Q, 16384),
                               ("dbg_hd", A_HDN, 22528)]:
            dma("sync", dbg_t[nm][:, :], arena[:, off:off + width],
                waits=allw, inc="oS0")
            allw = []

    ev("sync", lambda e: e.wait_ge(sems["oS0"], cnt["oS0"]))
    ev("sync", lambda e: e.wait_ge(sems["oS1"], cnt["oS1"]))

    with es:
        with nc.Block() as blk:
            def mk_body(eng_nm):
                evs = EV[eng_nm]
                def body(e):
                    for waits, fn, incn, k in evs:
                        for snm, w in waits:
                            e.wait_ge(sems[snm], w)
                        instr = fn(e)
                        if incn is not None:
                            instr.then_inc(sems[incn], k)
                return body
            blk.gpsimd(mk_body("gpsimd"))
            blk.sync(mk_body("sync"))
            blk.tensor(mk_body("tensor"))
            blk.scalar(mk_body("scalar"))
            blk.vector(mk_body("vector"))
    return nc


def _prep_shared(W_qkv, b_qkv, W_proj, b_proj, gate_w, up_w, down_w,
                 gate_s, up_s, down_s):
    f32 = np.float32
    W_qkv = np.ascontiguousarray(np.asarray(W_qkv, f32))
    wq_t = np.ascontiguousarray(
        W_qkv.reshape(16, 128, 18, 128).transpose(2, 1, 0, 3)).reshape(
            18, 128, 2048)
    bqkv_t = np.ascontiguousarray(np.asarray(b_qkv, f32).reshape(18, 128).T)
    wp_t = np.ascontiguousarray(np.asarray(W_proj, f32)).reshape(16, 128, 2048)
    bproj_b = np.ascontiguousarray(
        np.broadcast_to(np.asarray(b_proj, f32), (128, 2048)))
    gt = np.ascontiguousarray(
        np.asarray(gate_w, f32).reshape(44, 128, 16, 128).transpose(0, 3, 2, 1)
    ).reshape(44, 128, 2048)
    ut = np.ascontiguousarray(
        np.asarray(up_w, f32).reshape(44, 128, 16, 128).transpose(0, 3, 2, 1)
    ).reshape(44, 128, 2048)
    gu_t = np.ascontiguousarray(np.concatenate([gt, ut], axis=2))
    gs_t = np.ascontiguousarray(np.asarray(gate_s, f32).reshape(44, 128).T)
    us_t = np.ascontiguousarray(np.asarray(up_s, f32).reshape(44, 128).T)
    dn = np.ascontiguousarray(np.asarray(down_w, f32).T).reshape(44, 128, 2048)
    dn_t = np.ascontiguousarray(
        dn.reshape(44, 128, 2, 1024).transpose(2, 0, 1, 3))
    dsb_b = np.ascontiguousarray(
        np.broadcast_to(np.asarray(down_s, f32), (128, 2048)))
    ident = np.eye(128, dtype=f32)
    return dict(wq_t=wq_t, bqkv_t=bqkv_t, wp_t=wp_t, bproj_b=bproj_b,
                gu_t=gu_t, gs_t=gs_t, us_t=us_t, dn_t=dn_t, dsb_b=dsb_b,
                ident=ident)


def _masks_for(sub):
    qstarts = (0, 1536) if sub == 0 else (512, 1024)
    m = np.zeros((128, 16 * 512), dtype=np.float32)
    kk = np.arange(128)[:, None]
    q = np.arange(512)[None, :]
    for g in range(2):
        for j in range(8):
            kc = j if g == 0 else 8 + j
            valid = (kc * 128 + kk) <= (qstarts[g] + q)
            m[:, (g * 8 + j) * 512:(g * 8 + j + 1) * 512] = \
                valid.astype(np.float32)
    return m


def _in_maps(x, shared):
    masks = [_masks_for(0), _masks_for(1)]
    in_maps, own_all = [], []
    for c in range(8):
        b, sub = c // 2, c % 2
        qs = (0, 1536) if sub == 0 else (512, 1024)
        own = np.r_[qs[0]:qs[0] + 512, qs[1]:qs[1] + 512]
        own_all.append(own)
        xb = x[b]
        im = dict(shared)
        im["xf_g"] = np.ascontiguousarray(xb.T)
        im["xtm_g"] = np.ascontiguousarray(xb)
        im["xf_o"] = np.ascontiguousarray(xb[own].T)
        im["xtm_o"] = np.ascontiguousarray(xb[own])
        im["masks"] = masks[sub]
        in_maps.append(im)
    return in_maps, own_all


def kernel(L0_attn_x, L0_attn_W_qkv, L0_attn_b_qkv, L0_attn_W_proj,
           L0_attn_b_proj, L0_ffn_gate_w, L0_ffn_up_w, L0_ffn_down_w,
           L0_ffn_gate_s, L0_ffn_up_s, L0_ffn_down_s):
    f32 = np.float32
    x = np.ascontiguousarray(np.asarray(L0_attn_x, f32))
    shared = _prep_shared(L0_attn_W_qkv, L0_attn_b_qkv, L0_attn_W_proj,
                          L0_attn_b_proj, L0_ffn_gate_w, L0_ffn_up_w,
                          L0_ffn_down_w, L0_ffn_gate_s, L0_ffn_up_s,
                          L0_ffn_down_s)
    nc = build()
    in_maps, own_all = _in_maps(x, shared)
    res = run_bass_kernel_spmd(nc, in_maps, core_ids=list(range(8)))
    out = np.empty((B, S, H), dtype=f32)
    for c in range(8):
        out[c // 2][own_all[c]] = np.asarray(res.results[c]["out"])
    return out
